# version 15
# baseline (speedup 1.0000x reference)
import numpy as np
import ml_dtypes
from contextlib import ExitStack
import bass_rust
import concourse.bass as bass
import concourse.mybir as mybir
from concourse.bass_utils import run_bass_kernel_spmd
from concourse.alu_op_type import AluOpType as ALU

F32 = mybir.dt.float32
BF16 = mybir.dt.bfloat16
FP16 = mybir.dt.float16
AF = mybir.ActivationFunctionType

EPS_A = 1e-4
LN_EPS = 1e-5
BIG = 1e9
PRIME = 1 << 16

CFG_FULL = dict(NSEQ=4, T=2048, D=1024, H=16, HD=64, E=16, EH=256, HS=2048)

CONSTS = [0.0, 1.0, -1.0, 2.0, -2.0, 0.25, 0.5, EPS_A, 2 * EPS_A, EPS_A * EPS_A,
          LN_EPS, 1e-20, BIG, -BIG]


def dsl(base, i, w):
    if isinstance(i, int):
        s0 = base + i * w
        return slice(s0, s0 + w)
    return bass_rust.DynSlice(i * w + base, w)


def dsl2(base, i, step, w):
    if isinstance(i, int):
        s0 = base + i * step
        return slice(s0, s0 + w)
    return bass_rust.DynSlice(i * step + base, w)


class P:

    ENG = ("T", "A", "V", "G", "S")

    def __init__(self, nc, ctx):
        self.nc = nc
        self.ctx = ctx
        self.eng = {"T": nc.tensor, "A": nc.scalar, "V": nc.vector,
                    "G": nc.gpsimd, "S": nc.sync}
        self.sems = {}
        self.cnt = {}
        self.regs = {}
        self.tmp = {e: self.eng[e].alloc_register(f"tmp_{e}") for e in self.ENG}
        self.loops = []

    def sem(self, name, waiters=("T", "A", "V", "G", "S"), prime=True):
        h = self.ctx.enter_context(self.nc.semaphore(name))
        self.sems[name] = h
        for e in waiters:
            r = self.eng[e].alloc_register(f"r_{name}_{e}")
            self.eng[e].reg_mov(r, 0)
            self.regs[(name, e)] = r
        if prime:
            self.eng["S"].sem_inc(h, PRIME)
            self.cnt[name] = PRIME
        else:
            self.cnt[name] = 0
        return h

    def inc(self, instr, name, n=1):
        instr.then_inc(self.sems[name], n)
        self.cnt[name] += n
        return instr

    def wait(self, e, name, lag=0):
        tgt = self.cnt[name] + lag
        eng = self.eng[e]
        r = self.regs[(name, e)]
        eng.reg_add(self.tmp[e], r, tgt)
        eng.wait_ge(self.sems[name], self.tmp[e])

    def loop(self, n):
        prog = self

        class _L:
            def __enter__(sl):
                sl.fori = prog.nc.Fori(0, n)
                i = sl.fori.__enter__()
                prog.loops.append({s: prog.cnt[s] for s in prog.sems})
                return i

            def __exit__(sl, *a):
                entry = prog.loops.pop()
                deltas = {s: prog.cnt[s] - entry[s] for s in prog.sems}
                for (s, e), r in prog.regs.items():
                    if deltas[s]:
                        prog.eng[e].reg_add(r, r, deltas[s])
                sl.fori.__exit__(*a)
                for (s, e), r in prog.regs.items():
                    if deltas[s]:
                        prog.eng[e].reg_add(r, r, -deltas[s] * n)
                for s, d in deltas.items():
                    if d:
                        prog.cnt[s] += d * (n - 1)
        return _L()


def build_program(cfg, use_repl_dma=False, dump_internal=False):
    NSEQ, T, D, H, HD, E, EH, HS = (cfg[k] for k in
                                    ("NSEQ", "T", "D", "H", "HD", "E", "EH", "HS"))
    DC = D // 128
    HP = H // 2
    SC = T // 128
    T4 = T // 512
    TJ = T // 1024
    HHD = H * HD
    HC = HHD // 128
    MB = HS // 128
    EC = EH // 128
    assert E == 16 and H % 2 == 0 and T % 1024 == 0 and D % 128 == 0
    assert HD == 64 and EH % 128 == 0 and HS % 128 == 0 and HHD % 128 == 0

    QKG_COLS = HP * 3 * D
    V_COLS = DC * HHD
    OUT_COLS = HC * D
    RW_COLS = DC * 16
    SEC1 = MB * 2 * D
    EPIECE = 2 * DC * EH
    SEC2 = E * EPIECE
    SEC3C = HS + E * EC * 128
    SEC3 = DC * SEC3C
    FFN_COLS = SEC1 + SEC2 + SEC3

    nc = bass.Bass("TRN2", target_bir_lowering=False, debug=False,
                   detect_race_conditions=False)
    ap = lambda t: t.ap()
    x_in = ap(nc.dram_tensor("x_in", [NSEQ * D, T], F32, kind="ExternalInput"))
    wf_qkg = ap(nc.dram_tensor("wf_qkg", [128, QKG_COLS], F32, kind="ExternalInput"))
    wf_v = ap(nc.dram_tensor("wf_v", [128, V_COLS], F32, kind="ExternalInput"))
    wf_out = ap(nc.dram_tensor("wf_out", [128, OUT_COLS], F32, kind="ExternalInput"))
    wf_rw = ap(nc.dram_tensor("wf_rw", [128, RW_COLS], F32, kind="ExternalInput"))
    wf_ffn = ap(nc.dram_tensor("wf_ffn", [128, FFN_COLS], F32, kind="ExternalInput"))
    bqkg_col = ap(nc.dram_tensor("bqkg_col", [128, HP * 3], F32,
                                 kind="ExternalInput"))
    bv_bc = ap(nc.dram_tensor("bv_bc", [128, HHD], F32, kind="ExternalInput"))
    bo_col = ap(nc.dram_tensor("bo_col", [128, DC], F32, kind="ExternalInput"))
    cb_bc = ap(nc.dram_tensor("cb_bc", [128, 64], F32, kind="ExternalInput"))
    idm32 = ap(nc.dram_tensor("idm32", [128, 128], F32, kind="ExternalInput"))
    y_out = ap(nc.dram_tensor("y_out", [NSEQ * D, T], F32, kind="ExternalOutput"))
    wb_qkg = ap(nc.dram_tensor("wb_qkg", [128, QKG_COLS], BF16))
    wb_v = ap(nc.dram_tensor("wb_v", [128, V_COLS], BF16))
    wb_out = ap(nc.dram_tensor("wb_out", [128, OUT_COLS], BF16))
    wb_rw = ap(nc.dram_tensor("wb_rw", [128, RW_COLS], BF16))
    wb_ffn = ap(nc.dram_tensor("wb_ffn", [128, FFN_COLS], BF16))
    _ik = dict(kind="ExternalOutput") if dump_internal else {}
    h_hbm = ap(nc.dram_tensor("h_hbm", [HHD, T], BF16, **_ik))
    x1_hbm = ap(nc.dram_tensor("x1_hbm", [D, T], F32, **_ik))
    if dump_internal:
        dbg16 = ap(nc.dram_tensor("dbg16", [128, 40960], BF16, kind="ExternalOutput"))
        dbg32 = ap(nc.dram_tensor("dbg32", [128, 1024], F32, kind="ExternalOutput"))

    with ExitStack() as ctx:
        En = ctx.enter_context
        p = P(nc, ctx)
        TT, AA, VV, GG, SS = nc.tensor, nc.scalar, nc.vector, nc.gpsimd, nc.sync

        cp32 = En(nc.sbuf_tensor("cp32", [128, len(CONSTS)], F32))
        cp16 = En(nc.sbuf_tensor("cp16", [128, len(CONSTS)], BF16))
        ones_r16 = En(nc.sbuf_tensor("ones_r16", [128, 128], BF16))
        ones_r32 = En(nc.sbuf_tensor("ones_r32", [128, 128], F32))
        ones_c16 = En(nc.sbuf_tensor("ones_c16", [128, 2], BF16))
        onesw = En(nc.sbuf_tensor("onesw", [128, 2048], BF16))
        idt32 = En(nc.sbuf_tensor("idt32", [128, 128], F32))
        bqkg = En(nc.sbuf_tensor("bqkg", [128, HP * 3], F32))
        bstg = En(nc.sbuf_tensor("bstg", [128, 4], F32))
        bvb = En(nc.sbuf_tensor("bvb", [128, HHD], F32))
        boc = En(nc.sbuf_tensor("boc", [128, DC], F32))
        cbb = En(nc.sbuf_tensor("cbb", [128, 64], F32))
        rww = En(nc.sbuf_tensor("rww", [128, RW_COLS], BF16))
        xbn = En(nc.sbuf_tensor("xbn", [128, DC * T], BF16))

        XF_O = 0
        ST_O = XF_O + DC * 512
        X1_O = ST_O + 2048
        OB_O = X1_O
        RT_O = OB_O + DC * 512
        LG_O = OB_O
        AF32_COLS = max(RT_O + 384, 2 * 4096)
        ar32 = En(nc.sbuf_tensor("ar32", [128, AF32_COLS], F32))

        o = 0
        VAUG_O = o; o += HP * SC * 130
        assert VAUG_O == 0
        QR_O = [o, o + T]; o += 2 * T
        KL_O = [o, o + T]; o += 2 * T
        GH_O = [o, o + T]; o += 2 * T
        PSCR_O = o; o += 3 * T
        SQ_O = o; o += T
        ROW_O = o; o += max(2 * T, DC * 512 - T + 1024)
        RB_O = [o, o + 1024]; o += 2048
        WT_O = [o, o + 1024]; o += 2048
        ZB_OFF = None
        HST_O = o; o += T
        RHO_O = o; o += 1024
        RHOB_O = o; o += 512
        AEV_O = o; o += 512
        WPAIR_O = [o, o]; o += 3 * D
        WBIG_O = o; o += max(V_COLS, OUT_COLS)
        WT_O.append(WBIG_O)
        VSTG_O = [o, o]; o += SC * 130
        ZB_OFF = [SQ_O, ROW_O + 2048, RHO_O]
        att_cols = o
        o = 0
        SH_O = o; o += MB * 512
        assert SH_O == 0
        HW_O = o; o += E * EC * 512
        assert HW_O % 512 == 0
        MSG_O = [o, o + 2 * D]; o += 4 * D
        EB_O = [o, o + EPIECE]; o += 2 * EPIECE
        CBF_O = [o, o + SEC3C]; o += 2 * SEC3C
        BCALL_O = o; o += E * 512
        CMB16_O = o; o += 64
        CMBF_O = o; o += 512
        FT1_O = o; o += EC * 512
        XSTG_O = o; o += DC * 512
        ffn_cols = o
        AB16_COLS = max(att_cols, ffn_cols, 2 * 8192)
        ar16 = En(nc.sbuf_tensor("ar16", [128, AB16_COLS], BF16))

        def a32(start, n):
            return ar32[:, start:start + n]

        def a16(start, n):
            return ar16[:, start:start + n]

        def warm(psbank):
            TT.matmul(psbank[0:1, 0:32], ones_c16[0:1, 0:1], onesw[0:1, 0:32],
                      start=True, stop=True)

        pd2a = En(nc.psum_tensor("pd2a", [128, 1024], F32))
        pd2b = En(nc.psum_tensor("pd2b", [128, 1024], F32))
        pA0 = En(nc.psum_tensor("pA0", [128, 512], F32))
        pA1 = En(nc.psum_tensor("pA1", [128, 512], F32))
        pX = En(nc.psum_tensor("pX", [128, 512], F32))
        pY = En(nc.psum_tensor("pY", [128, 512], F32))

        p.sem("sT", waiters=("A", "V", "G", "S"))
        p.sem("sA", waiters=("T", "A", "V", "G", "S"))
        p.sem("sV", waiters=("T", "A", "V", "G", "S"))
        p.sem("dInit", waiters=("T", "V"), prime=False)
        p.sem("dCf0", waiters=("V",), prime=False); p.sem("dCf1", waiters=("V",), prime=False)
        p.sem("dCb0", waiters=("V", "S"), prime=False); p.sem("dCb1", waiters=("V", "S"), prime=False)
        p.sem("dXf", waiters=("V",), prime=False)
        p.sem("dHb", waiters=("T",), prime=False)
        p.sem("dWbig", waiters=("T",), prime=False)
        p.sem("dWp0", waiters=("T",), prime=False); p.sem("dWp1", waiters=("T",), prime=False)
        p.sem("dMsg0", waiters=("T",), prime=False); p.sem("dMsg1", waiters=("T",), prime=False)
        p.sem("dEb0", waiters=("T",), prime=False); p.sem("dEb1", waiters=("T",), prime=False)
        p.sem("dCb20", waiters=("T",), prime=False); p.sem("dCb21", waiters=("T",), prime=False)
        p.sem("dShift", waiters=("T", "V"), prime=False)
        p.sem("dRho", waiters=("T", "V"), prime=False)
        p.sem("dRow", waiters=("T",), prime=False)
        p.sem("dH", waiters=("S",), prime=False)
        p.sem("dX1", waiters=("S", "V"), prime=False)
        p.sem("dY", waiters=("G", "V", "A"), prime=False)
        p.sem("dVstg", waiters=("T",), prime=False)

        def dmaS(sem, out, in_, **kw):
            p.inc(SS.dma_start(out=out, in_=in_, **kw), sem, 16)

        def dmaG(sem, out, in_, **kw):
            p.inc(GG.dma_start(out=out, in_=in_, **kw), sem, 16)

        C32 = {v: cp32[:, i:i + 1] for i, v in enumerate(CONSTS)}
        for i, v in enumerate(CONSTS):
            nc.const_aps.register(F32, v, C32[v])
            nc.const_aps.register(BF16, v, cp16[:, i:i + 1])

        for i, v in enumerate(CONSTS):
            VV.memset(cp32[:, i:i + 1], v)
            VV.memset(cp16[:, i:i + 1], v)
        VV.memset(ones_r16[0:1, :], 1.0)
        VV.memset(ones_r16[64:65, :], 1.0)
        VV.memset(ones_r32[0:1, :], 1.0)
        VV.memset(ones_r32[64:65, :], 1.0)
        VV.memset(ones_c16[:, :], 1.0)
        VV.memset(ar16[:, att_cols:att_cols + 64], 0.0)
        p.inc(VV.memset(onesw[:, :], 1.0), "sV")
        dmaS("dInit", idt32[:], idm32)
        dmaS("dInit", bqkg[:], bqkg_col); dmaS("dInit", bvb[:], bv_bc)
        dmaS("dInit", boc[:], bo_col); dmaS("dInit", cbb[:], cb_bc)

        CH = 4096
        cast_jobs = []
        for src, dst, cols in ((wf_qkg, wb_qkg, QKG_COLS), (wf_v, wb_v, V_COLS),
                               (wf_out, wb_out, OUT_COLS), (wf_rw, wb_rw, RW_COLS),
                               (wf_ffn, wb_ffn, FFN_COLS)):
            c0 = 0
            while c0 < cols:
                w = min(CH, cols - c0)
                cast_jobs.append((src, dst, c0, w))
                c0 += w
        for j, (src, dst, c0, w) in enumerate(cast_jobs):
            par = j % 2
            fbuf = a32(par * 4096, w)
            bbuf = a16(par * 4096, w)
            if par == 0:
                if j >= 2:
                    p.wait("S", "sV", -1)
                dmaS(f"dCf{par}", fbuf, src[:, c0:c0 + w])
            else:
                if j >= 2:
                    p.wait("A", "sV", -1)
                p.inc(AA.dma_start(out=fbuf, in_=src[:, c0:c0 + w]),
                      f"dCf{par}", 16)
            p.wait("V", f"dCf{par}")
            if j >= 2:
                p.wait("V", f"dCb{par}")
            p.inc(VV.tensor_copy(bbuf, fbuf), "sV")
            p.wait("G", "sV")
            dmaG(f"dCb{par}", dst[:, c0:c0 + w], bbuf)
        p.wait("S", "dCb0")
        p.wait("S", "dCb1")
        dmaS("dInit", rww[:], wb_rw)
        p.wait("T", "dInit")
        p.wait("V", "dInit")
        nc.all_engine_barrier()

        def xsl(k, tt, width=512):
            return xbn[:, dsl(k * T, tt, width)] if width == 512 else \
                xbn[:, k * T + tt * 1024: k * T + (tt + 1) * 1024]

        def ln_phase(src_tile):
            XO = (XF_O, X1_O)
            LSEM = ("dXf", "dRho")

            def ld(tt):
                p.wait("S", "sV")
                if tt <= 1:
                    p.wait("S", "dX1")
                dmaS(LSEM[tt % 2],
                     a32(XO[tt % 2], DC * 512).rearrange("q (k w) -> q k w",
                                                         k=DC), src_tile(tt))
            ld(0)
            for tt in range(T4):
                xf = a32(XO[tt % 2], DC * 512)
                p.wait("V", LSEM[tt % 2])
                for k in range(DC):
                    ins = VV.tensor_copy(xsl(k, tt), xf[:, k * 512:(k + 1) * 512])
                p.inc(ins, "sV")
                if tt + 1 < T4:
                    ld(tt + 1)
                sqb = a16(SQ_O, DC * 512)
                for k in range(DC):
                    ins = VV.tensor_tensor(sqb[:, k * 512:(k + 1) * 512],
                                           xsl(k, tt), xsl(k, tt), op=ALU.mult)
                p.inc(ins, "sV")
                p.wait("T", "sV")
                for k in range(DC):
                    TT.matmul(pX[0:1, :], ones_c16[:, 0:1], xsl(k, tt),
                              start=(k == 0), stop=(k == DC - 1))
                for k in range(DC):
                    ins = TT.matmul(pY[0:1, :], ones_c16[:, 0:1],
                                    sqb[:, k * 512:(k + 1) * 512],
                                    start=(k == 0), stop=(k == DC - 1))
                p.inc(ins, "sT")
                st = a32(ST_O, 2048)
                p.wait("V", "sT")
                VV.tensor_scalar(st[0:1, 0:512], pX[0:1, :], 1.0 / D, None,
                                 op0=ALU.mult)
                VV.tensor_tensor(st[0:1, 512:1024], st[0:1, 0:512], st[0:1, 0:512],
                                 op=ALU.mult)
                p.inc(VV.scalar_tensor_tensor(st[0:1, 1024:1536], pY[0:1, :],
                                              1.0 / D, st[0:1, 512:1024],
                                              op0=ALU.mult, op1=ALU.subtract), "sV")
                p.wait("A", "sV")
                p.inc(AA.activation(st[0:1, 1536:2048], st[0:1, 1024:1536], AF.Ln,
                                    bias=LN_EPS, scale=1.0), "sA")
                p.wait("A", "sA", -1)
                p.inc(AA.activation(st[0:1, 1024:1536], st[0:1, 1536:2048], AF.Exp,
                                    scale=-0.5), "sA")
                p.wait("V", "sA")
                mrow = a16(SQ_O + DC * 512, 512)
                p.inc(VV.tensor_copy(mrow[0:1, :], st[0:1, 0:512]), "sV")
                p.wait("T", "sV")
                TT.matmul(pX[:, :], ones_r16[0:1, :], mrow[0:1, :],
                          start=True, stop=True)
                p.inc(TT.matmul(pY[:, :], ones_r32[0:1, :], st[0:1, 1024:1536],
                                start=True, stop=True), "sT")
                p.wait("A", "sT")
                mbb = a16(SQ_O + DC * 512 + 512, 512)
                p.inc(AA.activation(mbb[:, :], pX[:, :], AF.Copy), "sA")
                p.wait("V", "sA")
                for k in range(DC):
                    VV.tensor_tensor(xsl(k, tt), xsl(k, tt), mbb[:, :],
                                     op=ALU.subtract)
                    ins = VV.tensor_tensor(xsl(k, tt), xsl(k, tt), pY[:, :],
                                           op=ALU.mult)
                p.inc(ins, "sV")
            nc.all_engine_barrier()

        def emit_A(sc, tj, vbase, u, vlag=0):
            wt = a16(WT_O[sc % 3], 1024)
            p.wait("T", "sV", vlag)
            lh = ar16[:, vbase + sc * 130 + u * 65:
                      vbase + sc * 130 + u * 65 + 128]
            TT.matmul(pA0[0:128, :], lh, wt[:, 0:512],
                      start=(sc == 0), stop=(sc == SC - 1))
            p.inc(TT.matmul(pA1[0:128, :], lh, wt[:, 512:1024],
                            start=(sc == 0), stop=(sc == SC - 1)), "sT")

        with p.loop(NSEQ) as s:
            nc.all_engine_barrier()
            x4 = x_in.rearrange("(sq k q) t -> sq q k t", q=128, k=DC)
            ln_phase(lambda tt: x4[dsl(0, s, 1), :, :, tt * 512:(tt + 1) * 512])

            for bo in (QR_O[0], QR_O[1], KL_O[0], KL_O[1]):
                ins = VV.memset(ar16[64:128, bo:bo + T], 0.0)
            p.inc(ins, "sV")

            wv = a16(WBIG_O, V_COLS)
            p.wait("S", "sT")
            dmaS("dWbig", wv, wb_v)
            vaug_full = a16(VAUG_O, HP * SC * 130)
            vview = vaug_full.rearrange("q (hp sc u d) -> q hp sc u d",
                                        hp=HP, sc=SC, u=2)
            p.inc(VV.memset(vview[:, :, :, :, 64:65], 1.0), "sV")
            p.wait("T", "dWbig")
            p.wait("T", "sV")
            NHALF = HHD // 512 if HHD >= 512 else 1
            HW_HALF = min(HHD, 512)
            grp_i = 0
            for st_ in range(SC):
                for half in range(NHALF):
                    ps = pX if grp_i % 2 == 0 else pY
                    if grp_i >= 2:
                        p.wait("T", "sV", -1)
                    for k in range(DC):
                        ins = TT.matmul(ps[:, 0:HW_HALF],
                                        xbn[:, k * T + st_ * 128:k * T + st_ * 128 + 128],
                                        wv[:, k * HHD + half * 512:
                                           k * HHD + half * 512 + HW_HALF],
                                        start=(k == 0), stop=(k == DC - 1))
                    p.inc(ins, "sT")
                    p.wait("V", "sT")
                    nhp = HW_HALF // 128
                    hp0 = half * 4
                    outv = vview[:, hp0:hp0 + nhp, st_, :, 0:64]
                    inv = ps[:, 0:HW_HALF].rearrange("q (hp u d) -> q hp u d",
                                                     hp=nhp, u=2)
                    bvv = bvb[:, half * 512:half * 512 + HW_HALF].rearrange(
                        "q (hp u d) -> q hp u d", hp=nhp, u=2)
                    p.inc(VV.tensor_tensor(outv, inv, bvv, op=ALU.add), "sV")
                    grp_i += 1
            nc.all_engine_barrier()

            with p.loop(HP // 2) as hp2:
                for u2 in range(2):
                    hp = hp2 * 2 + u2
                    wp = a16(WPAIR_O[u2], 3 * D)
                    p.wait("S", "sT", 0)
                    dmaS(f"dWp{u2}", wp, wb_qkg[:, dsl(0, hp, 3 * D)])
                    vstg = a16(VSTG_O[u2], SC * 130)
                    p.wait("V", "sT")
                    p.inc(VV.tensor_copy(bstg[:, 0:3], bqkg[:, dsl(0, hp, 3)]), "sV")
                    p.inc(VV.tensor_copy(vstg, ar16[:, dsl(0, hp, SC * 130)]), "sV")
                    p.wait("V", "sV")
                    p.wait("T", f"dWp{u2}")
                    gi = 0
                    for si, (dst0, scl) in enumerate(
                            ((QR_O, -2.0), (KL_O, 1.0), (GH_O, 1.0))):
                        for tt in range(T4):
                            ps = pX if gi % 2 == 0 else pY
                            p.wait("T", "sV", -1 if gi >= 2 else -2)
                            for k in range(DC):
                                ins = TT.matmul(
                                    ps[:, :],
                                    wp[:, si * D + k * 128:si * D + (k + 1) * 128],
                                    xsl(k, tt), start=(k == 0), stop=(k == DC - 1))
                            p.inc(ins, "sT")
                            p.wait("V", "sT")
                            bcl = bstg[:, si:si + 1]
                            d0 = ar16[0:64, dst0[0] + tt * 512:dst0[0] + (tt + 1) * 512]
                            d1 = ar16[64:128, PSCR_O + si * T + tt * 512:
                                      PSCR_O + si * T + (tt + 1) * 512]
                            if scl == 1.0:
                                VV.tensor_scalar(d0, ps[0:64, :], bcl[0:64, :], None,
                                                 op0=ALU.add)
                                p.inc(VV.tensor_scalar(d1, ps[64:128, :],
                                                       bcl[64:128, :], None,
                                                       op0=ALU.add), "sV")
                            else:
                                VV.tensor_scalar(d0, ps[0:64, :], bcl[0:64, :], scl,
                                                 op0=ALU.add, op1=ALU.mult)
                                p.inc(VV.tensor_scalar(d1, ps[64:128, :],
                                                       bcl[64:128, :], scl,
                                                       op0=ALU.add, op1=ALU.mult),
                                      "sV")
                            gi += 1
                    p.wait("G", "sV")
                    dmaG("dShift", ar16[0:64, QR_O[1]:QR_O[1] + T],
                         ar16[64:128, PSCR_O:PSCR_O + T])
                    dmaG("dShift", ar16[0:64, KL_O[1]:KL_O[1] + T],
                         ar16[64:128, PSCR_O + T:PSCR_O + 2 * T])
                    dmaG("dShift", ar16[0:64, GH_O[1]:GH_O[1] + T],
                         ar16[64:128, PSCR_O + 2 * T:PSCR_O + 3 * T])
                    for uu in range(2):
                        dmaG("dShift", ar16[66:68, QR_O[uu]:QR_O[uu] + T],
                             onesw[0:2, 0:T])
                        dmaG("dShift", ar16[64:66, KL_O[uu]:KL_O[uu] + T],
                             onesw[0:2, 0:T])

                    for u in range(2):
                        qr = ar16[:, QR_O[u]:QR_O[u] + T]
                        kl = ar16[:, KL_O[u]:KL_O[u] + T]
                        gh = ar16[:, GH_O[u]:GH_O[u] + T]
                        if u == 1:
                            p.wait("V", "dShift")
                        sqs = a16(SQ_O, T)
                        rows = a16(ROW_O, 2 * T)
                        for (tile, qscale, hoff) in ((qr, 0.25, 0), (kl, 1.0, 1)):
                            p.inc(VV.tensor_tensor(sqs[0:64, :], tile[0:64, :],
                                                   tile[0:64, :], op=ALU.mult), "sV")
                            p.wait("T", "sV")
                            for tt in range(T4):
                                pdx = pd2a if tt < 2 else pd2b
                                col = (tt % 2) * 512
                                ins = TT.matmul(pdx[0:1, col:col + 512],
                                                ones_c16[0:64, 0:1],
                                                sqs[0:64, tt * 512:(tt + 1) * 512],
                                                start=True, stop=True)
                            p.inc(ins, "sT")
                            p.wait("A", "sT")
                            W1 = min(T4, 2) * 512
                            ins = AA.activation(rows[0:1, 0:W1], pd2a[0:1, 0:W1],
                                                AF.Copy, scale=qscale)
                            if T4 > 2:
                                ins = AA.activation(rows[0:1, 1024:2048],
                                                    pd2b[0:1, :], AF.Copy,
                                                    scale=qscale)
                            p.inc(ins, "sA")
                            p.wait("V", "sA")
                            ins = VV.scalar_tensor_tensor(rows[0:1, T:T + W1],
                                                          pd2a[0:1, 0:W1], qscale,
                                                          rows[0:1, 0:W1],
                                                          op0=ALU.mult,
                                                          op1=ALU.subtract)
                            if T4 > 2:
                                ins = VV.scalar_tensor_tensor(
                                    rows[0:1, T + 1024:T + 2048], pd2b[0:1, :],
                                    qscale, rows[0:1, 1024:2048],
                                    op0=ALU.mult, op1=ALU.subtract)
                            p.inc(ins, "sV")
                            p.wait("G", "sV")
                            if hoff == 0:
                                dmaG("dShift", qr[64:65, :], rows[0:1, 0:T])
                                dmaG("dShift", qr[65:66, :], rows[0:1, T:2 * T])
                            else:
                                dmaG("dShift", kl[66:67, :], rows[0:1, 0:T])
                                dmaG("dShift", kl[67:68, :], rows[0:1, T:2 * T])
                        p.wait("T", "dShift")

                        vau = a16(VSTG_O[u2], SC * 130)
                        if u == 0:
                            p.wait("T", "sV")
                        pend_rho = [None]
                        for tj in range(TJ):
                            def stage_T(sc):
                                par = sc % 2
                                pd = pd2a if par == 0 else pd2b
                                if sc == 0:
                                    p.wait("V", "dShift")
                                p.wait("T", "sA", 0 if sc <= 1 else -2)
                                if sc <= 1:
                                    p.wait("T", "sV")
                                TT.matmul(pd[:, 0:512],
                                          kl[0:128, sc * 128:(sc + 1) * 128],
                                          qr[0:128, tj * 1024:tj * 1024 + 512],
                                          start=True, stop=True)
                                p.inc(TT.matmul(
                                    pd[:, 512:1024], kl[0:128, sc * 128:(sc + 1) * 128],
                                    qr[0:128, tj * 1024 + 512:tj * 1024 + 1024],
                                    start=True, stop=True), "sT")

                            def stage_ACT(sc, skew):
                                if sc < SC:
                                    par = sc % 2
                                    pd = pd2a if par == 0 else pd2b
                                    lb = ar16[:, RB_O[par]:RB_O[par] + 1024]\
                                        .bitcast(FP16)[:, 0:1024]
                                    p.wait("A", "sT", -1 if skew else 0)
                                    p.inc(AA.activation(lb, pd[:, :], AF.Ln,
                                                        bias=EPS_A * EPS_A,
                                                        scale=1.0), "sA")
                                if sc >= 1:
                                    parp = (sc - 1) % 2
                                    lbp = ar16[:, RB_O[parp]:RB_O[parp] + 1024]\
                                        .bitcast(FP16)[:, 0:1024]
                                    zb = a16(ZB_OFF[(sc - 1) % 3], 1024)
                                    p.inc(AA.activation(zb, lbp, AF.Exp,
                                                        scale=-0.5), "sA")

                            def stage_V(sc, tlag=-2):
                                par = sc % 2
                                zb = a16(ZB_OFF[sc % 3], 1024)
                                bp = a16(ROW_O + par * 1024, 1024)
                                wt = a16(WT_O[sc % 3], 1024)
                                p.wait("V", "sA")
                                p.wait("V", "sT", tlag)
                                with nc.allow_low_precision(reason="bf16 attn wts"):
                                    VV.tensor_scalar(bp, zb, -EPS_A, 1.0,
                                                     op0=ALU.mult, op1=ALU.add)
                                    VV.tensor_tensor(bp, zb, bp, op=ALU.mult)
                                    p.inc(VV.tensor_tensor(wt, bp, bp, op=ALU.mult),
                                          "sV")

                            for sc in range(SC):
                                stage_T(sc)
                                if sc >= 3:
                                    emit_A(sc - 3, tj, VSTG_O[u2], u, vlag=-1)
                                stage_ACT(sc, skew=(sc >= 3))
                                if sc >= 1:
                                    stage_V(sc - 1)
                                if sc == 1 and pend_rho[0] is not None:
                                    pend_rho[0]()
                                    pend_rho[0] = None
                            p.wait("A", "sV", -1)
                            stage_ACT(SC, skew=True)
                            stage_V(SC - 1, tlag=0)
                            emit_A(SC - 3, tj, VSTG_O[u2], u, vlag=-2)
                            emit_A(SC - 2, tj, VSTG_O[u2], u, vlag=-1)
                            emit_A(SC - 1, tj, VSTG_O[u2], u)
                            rho0 = a16(RHO_O, 1024)
                            rho1 = a16(RHOB_O, 1024)
                            p.wait("A", "sT")
                            p.inc(AA.activation(rho0[64:65, 0:512], pA0[64:65, :],
                                                AF.Ln, bias=EPS_A, scale=1.0), "sA")
                            p.inc(AA.activation(rho1[64:65, 0:512], pA1[64:65, :],
                                                AF.Ln, bias=EPS_A, scale=1.0), "sA")
                            p.wait("A", "sA", -1)
                            p.inc(AA.activation(rho0[64:65, 512:1024],
                                                rho0[64:65, 0:512],
                                                AF.Exp, scale=-1.0), "sA")
                            p.wait("A", "sA", -1)
                            p.inc(AA.activation(rho1[64:65, 512:1024],
                                                rho1[64:65, 0:512],
                                                AF.Exp, scale=-1.0), "sA")
                            def mk_rho_tail(tj, rho0, rho1):
                                def tail():
                                    p.wait("T", "sV")
                                    p.wait("T", "sA", -1)
                                    p.inc(TT.matmul(pX[0:64, :],
                                                    ones_r16[64:65, 0:64],
                                                    rho0[64:65, 512:1024],
                                                    start=True, stop=True), "sT")
                                    p.wait("T", "sA")
                                    p.inc(TT.matmul(pY[0:64, :],
                                                    ones_r16[64:65, 0:64],
                                                    rho1[64:65, 512:1024],
                                                    start=True, stop=True), "sT")
                                    hs = lambda v: ar16[0:64,
                                                        HST_O + (tj * 2 + v) * 512:
                                                        HST_O + (tj * 2 + v + 1) * 512]
                                    ghs = lambda v: gh[0:64, tj * 1024 + v * 512:
                                                       tj * 1024 + (v + 1) * 512]
                                    p.wait("V", "sT", -2)
                                    VV.tensor_tensor(hs(0), pA0[0:64, :], ghs(0),
                                                     op=ALU.mult)
                                    p.wait("V", "sT", -1)
                                    VV.tensor_tensor(hs(0), hs(0), pX[0:64, :],
                                                     op=ALU.mult)
                                    VV.tensor_tensor(hs(1), pA1[0:64, :], ghs(1),
                                                     op=ALU.mult)
                                    p.wait("V", "sT")
                                    p.inc(VV.tensor_tensor(hs(1), hs(1),
                                                           pY[0:64, :],
                                                           op=ALU.mult), "sV")
                                return tail
                            tl = mk_rho_tail(tj, rho0, rho1)
                            if tj < TJ - 1:
                                pend_rho[0] = tl
                            else:
                                tl()
                        p.wait("G", "sV")
                        dmaG("dH", h_hbm[dsl(0, hp * 2 + u, HD), :],
                             ar16[0:64, HST_O:HST_O + T])
            nc.all_engine_barrier()

            wo = a16(WBIG_O, OUT_COLS)
            p.wait("S", "dH")
            p.wait("S", "sT")
            dmaS("dWbig", wo, wb_out)
            hv = h_hbm.rearrange("(k q) t -> q k t", q=128)
            x1v = x1_hbm.rearrange("(k q) t -> q k t", q=128)
            for tt in range(T4):
                xf = a32(XF_O, DC * 512)
                p.wait("S", "sV")
                dmaS("dXf", xf.rearrange("q (k w) -> q k w", k=DC),
                     x4[dsl(0, s, 1), :, :, tt * 512:(tt + 1) * 512])
                hbuf = a16(QR_O[0], HC * 512)
                p.wait("S", "sT")
                dmaS("dHb", hbuf.rearrange("q (k w) -> q k w", k=HC),
                     hv[:, :, tt * 512:(tt + 1) * 512])
                p.wait("T", "dWbig")
                p.wait("T", "dHb")
                x1sl = a32(X1_O, DC * 512)
                p.wait("V", "dX1")
                for ct in range(DC):
                    ps = pX if ct % 2 == 0 else pY
                    if ct >= 2:
                        p.wait("T", "sV", -1)
                    for k in range(HC):
                        ins = TT.matmul(ps[:, :],
                                        wo[:, k * D + ct * 128:k * D + (ct + 1) * 128],
                                        hbuf[:, k * 512:(k + 1) * 512],
                                        start=(k == 0), stop=(k == HC - 1))
                    p.inc(ins, "sT")
                    p.wait("V", "sT")
                    if ct == 0:
                        p.wait("V", "dXf")
                    p.inc(VV.scalar_tensor_tensor(
                        x1sl[:, ct * 512:(ct + 1) * 512], ps[:, :], boc[:, ct:ct + 1],
                        xf[:, ct * 512:(ct + 1) * 512], op0=ALU.add, op1=ALU.add),
                        "sV")
                p.wait("G", "sV")
                dmaG("dX1", x1v[:, :, tt * 512:(tt + 1) * 512],
                     x1sl.rearrange("q (k w) -> q k w", k=DC))
            nc.all_engine_barrier()

            p.wait("S", "dX1")
            ln_phase(lambda tt: x1v[:, :, tt * 512:(tt + 1) * 512])

            with p.loop(T4) as tt:
                xf = a32(XF_O, DC * 512)
                p.wait("S", "sV")
                dmaS("dXf", xf.rearrange("q (k w) -> q k w", k=DC),
                     x1v[:, :, dsl(0, tt, 512)])
                xstg = a16(XSTG_O, DC * 512)
                p.wait("V", "sT")
                xb4 = xbn[:].rearrange("q (k t) -> q k t", k=DC)
                p.inc(VV.tensor_copy(xstg.rearrange("q (k w) -> q k w", k=DC),
                                     xb4[:, :, dsl(0, tt, 512)]), "sV")

                def xst(k):
                    return xstg[:, k * 512:(k + 1) * 512]
                p.wait("T", "sA")
                p.wait("T", "sV")
                for k in range(DC):
                    ins = TT.matmul(pX[0:16, :], rww[:, k * 16:(k + 1) * 16],
                                    xst(k), start=(k == 0), stop=(k == DC - 1))
                p.inc(ins, "sT")
                lgf = a32(LG_O, 512)
                p.wait("A", "sT")
                p.wait("A", "dY")
                p.inc(AA.activation(lgf[0:16, :], pX[0:16, :], AF.Copy), "sA")
                rt = a32(RT_O, 384)
                p.wait("T", "sA")
                for tok in range(4):
                    if tok > 0:
                        p.wait("T", "sA")
                    p.inc(TT.transpose(pY[0:128, 0:16],
                                       lgf[0:16, tok * 128:(tok + 1) * 128],
                                       idt32[0:16, 0:16]), "sT")
                    p.wait("A", "sT")
                    p.inc(AA.activation(rt[:, tok * 16:(tok + 1) * 16],
                                        pY[0:128, 0:16], AF.Sigmoid), "sA")
                p.wait("V", "sA")
                _routing(p, VV, nc, rt, cbb[:, :])
                cmbf = a16(CMBF_O, 512)
                p.wait("T", "sV")
                for tok in range(4):
                    if tok > 0:
                        p.wait("T", "sA")
                    p.inc(TT.transpose(pY[0:16, 0:128],
                                       rt[:, 128 + tok * 16:128 + (tok + 1) * 16],
                                       idt32[:, :]), "sT")
                    p.wait("A", "sT")
                    p.inc(AA.activation(cmbf[0:16, tok * 128:(tok + 1) * 128],
                                        pY[0:16, 0:128], AF.Copy), "sA")
                bcall = a16(BCALL_O, E * 512)
                p.wait("G", "sA")
                if use_repl_dma:
                    for e in range(E):
                        dmaG("dRow", bcall[:, e * 512:(e + 1) * 512],
                             cmbf[e:e + 1, :].broadcast_to((128, 512)))
                else:
                    p.wait("G", "sA")
                    for e in range(E):
                        dmaG("dRow", bcall[64:65, e * 512:(e + 1) * 512],
                             cmbf[e:e + 1, :])
                    p.wait("T", "dRow")
                    for e in range(E):
                        if e > 0:
                            p.wait("T", "sA")
                        p.inc(TT.matmul(pY[:, :], ones_r16[64:65, :],
                                        bcall[64:65, e * 512:(e + 1) * 512],
                                        start=True, stop=True), "sT")
                        p.wait("A", "sT")
                        p.inc(AA.activation(bcall[:, e * 512:(e + 1) * 512],
                                            pY[:, :], AF.Copy), "sA")
                for m in range(MB):
                    pm = m % 2
                    mo = a16(MSG_O[pm], 2 * D)
                    if m >= 2:
                        p.wait("S", "sT", -2)
                    dmaS(f"dMsg{pm}", mo, wb_ffn[:, m * 2 * D:(m + 1) * 2 * D])
                    p.wait("T", f"dMsg{pm}")
                    p.wait("T", "sA")
                    for k in range(DC):
                        ins = TT.matmul(pX[:, :], mo[:, k * 128:(k + 1) * 128],
                                        xst(k), start=(k == 0),
                                        stop=(k == DC - 1))
                    p.inc(ins, "sT")
                    p.wait("T", "sV")
                    for k in range(DC):
                        ins = TT.matmul(pY[:, :],
                                        mo[:, D + k * 128:D + (k + 1) * 128],
                                        xst(k), start=(k == 0),
                                        stop=(k == DC - 1))
                    p.inc(ins, "sT")
                    ft1 = a16(FT1_O, 512)
                    p.wait("A", "sT", -1)
                    p.wait("A", "sV")
                    p.inc(AA.activation(ft1, pX[:, :], AF.Silu), "sA")
                    p.wait("V", "sA")
                    p.wait("V", "sT")
                    p.inc(VV.tensor_tensor(ar16[:, SH_O + m * 512:SH_O + (m + 1) * 512],
                                           ft1, pY[:, :], op=ALU.mult), "sV")
                for e in range(E):
                    pe = e % 2
                    eb = a16(EB_O[pe], EPIECE)
                    if e >= 2:
                        p.wait("S", "sT", -2)
                    dmaS(f"dEb{pe}", eb, wb_ffn[:, SEC1 + e * EPIECE:SEC1 + (e + 1) * EPIECE])
                    p.wait("T", f"dEb{pe}")
                    if pe == 0:
                        gb = (pd2a[:, 0:512], pd2a[:, 512:1024])
                        ub = (pd2b[:, 0:512], pd2b[:, 512:1024])
                        gsil = (pd2a[:, 0:EC * 512],)
                    else:
                        gb = (pA0[:, :], pA1[:, :])
                        ub = (pX[:, :], pY[:, :])
                        gsil = (pA0[:, :], pA1[:, :])[:EC]
                    if e >= 2:
                        p.wait("T", "sA", -1)
                    for mt in range(EC):
                        for k in range(DC):
                            ins = TT.matmul(
                                gb[mt], eb[:, k * EH + mt * 128:k * EH + (mt + 1) * 128],
                                xst(k), start=(k == 0), stop=(k == DC - 1))
                    p.inc(ins, "sT")
                    if e >= 2:
                        p.wait("T", "sV", -1)
                    for mt in range(EC):
                        for k in range(DC):
                            ins = TT.matmul(
                                ub[mt], eb[:, DC * EH + k * EH + mt * 128:
                                           DC * EH + k * EH + (mt + 1) * 128],
                                xst(k), start=(k == 0), stop=(k == DC - 1))
                    p.inc(ins, "sT")
                    ft1 = a16(FT1_O, EC * 512)
                    p.wait("A", "sT", -1)
                    p.wait("A", "sV")
                    if pe == 0:
                        p.inc(AA.activation(ft1, gsil[0], AF.Silu), "sA")
                    else:
                        for gi, gpc in enumerate(gsil):
                            ins = AA.activation(ft1[:, gi * 512:(gi + 1) * 512],
                                                gpc, AF.Silu)
                        p.inc(ins, "sA")
                    p.wait("V", "sA")
                    p.wait("V", "sT")
                    for mt in range(EC):
                        VV.tensor_tensor(ft1[:, mt * 512:(mt + 1) * 512],
                                         ft1[:, mt * 512:(mt + 1) * 512],
                                         ub[mt], op=ALU.mult)
                    for mt in range(EC):
                        ins = VV.tensor_tensor(
                            ar16[:, HW_O + (e * EC + mt) * 512:
                                 HW_O + (e * EC + mt + 1) * 512],
                            ft1[:, mt * 512:(mt + 1) * 512],
                            ar16[:, BCALL_O + e * 512:BCALL_O + (e + 1) * 512],
                            op=ALU.mult)
                    p.inc(ins, "sV")
                for ct in range(DC):
                    pc = ct % 2
                    cb = a16(CBF_O[pc], SEC3C)
                    if ct >= 2:
                        p.wait("S", "sT", -1)
                    dmaS(f"dCb2{pc}", cb, wb_ffn[:, SEC1 + SEC2 + ct * SEC3C:
                                                 SEC1 + SEC2 + (ct + 1) * SEC3C])
                    p.wait("T", f"dCb2{pc}")
                    pA = pA0 if pc == 0 else pA1
                    if ct < 2:
                        p.wait("T", "sV")
                    else:
                        p.wait("T", "sV", -1)
                    for m in range(MB):
                        TT.matmul(pA[:, :], cb[:, m * 128:(m + 1) * 128],
                                  ar16[:, SH_O + m * 512:SH_O + (m + 1) * 512],
                                  start=(m == 0), stop=False)
                    for e in range(E):
                        for mt in range(EC):
                            last = (e == E - 1 and mt == EC - 1)
                            ins = TT.matmul(
                                pA[:, :],
                                cb[:, HS + (e * EC + mt) * 128:
                                   HS + (e * EC + mt + 1) * 128],
                                ar16[:, HW_O + (e * EC + mt) * 512:
                                     HW_O + (e * EC + mt + 1) * 512],
                                start=False, stop=last)
                    p.inc(ins, "sT")
                    p.wait("V", "sT")
                    if ct == 0:
                        p.wait("V", "dXf")
                        p.wait("V", "dY")
                    p.inc(VV.tensor_tensor(
                        ar32[:, OB_O + ct * 512:OB_O + (ct + 1) * 512], pA[:, :],
                        ar32[:, XF_O + ct * 512:XF_O + (ct + 1) * 512],
                        op=ALU.add), "sV")
                p.wait("G", "sV")
                yv = y_out.rearrange("(sq k q) t -> sq q k t", q=128, k=DC)
                ob = a32(OB_O, DC * 512)
                dmaG("dY", yv[dsl(0, s, 1), :, :, dsl(0, tt, 512)],
                     ob.rearrange("q (k w) -> q k w", k=DC))
            nc.all_engine_barrier()

        if dump_internal:
            nd = 0
            for off, n in ((SH_O, MB * 512), (HW_O, E * EC * 512),
                           (BCALL_O, E * 512), (XSTG_O, DC * 512)):
                dmaG("dY", dbg16[:, nd:nd + n], ar16[:, off:off + n])
                nd += n
            dmaG("dY", dbg32[:, 0:384], ar32[:, RT_O:RT_O + 384])
            dmaG("dY", dbg32[:, 384:384 + 512], ar32[:, LG_O:LG_O + 512])
        p.wait("G", "dY")
    return nc


def _routing(p, VV, nc, rt, cbb):
    from concourse.alu_op_type import AluOpType as A

    def VS(ins):
        p.inc(ins, "sV")
        p.wait("V", "sV")
        return ins

    sc32 = rt[:, 0:64]
    sb = rt[:, 64:128]
    cur = rt[:, 128:192]
    gam = rt[:, 192:208]
    tA = rt[:, 208:212]; tB = rt[:, 212:216]
    ta = rt[:, 216:220]; tb = rt[:, 220:224]
    th = rt[:, 224:228]
    ge = rt[:, 228:232]; sel = rt[:, 232:236]; tmp1 = rt[:, 236:240]
    pen = rt[:, 240:256]
    mx = rt[:, 256:260]; dn = rt[:, 260:264]; rcp = rt[:, 264:268]
    msk = rt[:, 272:336]
    tre = rt[:, 336:368]
    r3 = lambda x: x.rearrange("q (r g j) -> q r g j", g=4, j=4)
    r2 = lambda x, g=4: x.rearrange("q (r g) -> q r g", g=g)
    sbv = r3(sb)
    VS(VV.tensor_tensor(sb, sc32, cbb, op=A.add))
    A_ = r2(tA, 1); B_ = r2(tB, 1); a_ = r2(ta, 1); b_ = r2(tb, 1)
    gamv = r2(gam)
    p1 = r2(pen)
    q1 = r2(msk[:, 0:16])
    p2 = r2(msk[:, 16:32])
    q2 = r2(msk[:, 32:48])
    VS(VV.tensor_tensor(p1, sbv[:, :, :, 0], sbv[:, :, :, 1], op=A.max))
    VS(VV.tensor_tensor(q1, sbv[:, :, :, 0], sbv[:, :, :, 1], op=A.min))
    VS(VV.tensor_tensor(p2, sbv[:, :, :, 2], sbv[:, :, :, 3], op=A.max))
    VS(VV.tensor_tensor(q2, sbv[:, :, :, 2], sbv[:, :, :, 3], op=A.min))
    VS(VV.tensor_tensor(gamv, p1, p2, op=A.add))
    VS(VV.tensor_tensor(p1, p1, q1, op=A.add))
    VS(VV.tensor_tensor(p2, p2, q2, op=A.add))
    VS(VV.tensor_tensor(p1, p1, p2, op=A.max))
    VS(VV.tensor_tensor(gamv, gamv, p1, op=A.max))
    g_ = lambda j: gamv[:, :, j:j + 1]
    VS(VV.tensor_tensor(A_, g_(0), g_(1), op=A.max))
    VS(VV.tensor_tensor(a_, g_(0), g_(1), op=A.min))
    VS(VV.tensor_tensor(B_, g_(2), g_(3), op=A.max))
    VS(VV.tensor_tensor(b_, g_(2), g_(3), op=A.min))
    thv = r2(th, 1)
    gev = r2(ge, 1); selv = r2(sel, 1); t1v = r2(tmp1, 1)
    VS(VV.tensor_tensor(thv, A_, B_, op=A.min))
    VS(VV.tensor_tensor(gev, A_, B_, op=A.is_ge))
    VS(VV.tensor_tensor(selv, gev, a_, op=A.mult))
    VS(VV.tensor_scalar(tmp1, ge, 1.0, -1.0, op0=A.subtract, op1=A.mult))
    VS(VV.tensor_tensor(t1v, t1v, b_, op=A.mult))
    VS(VV.tensor_tensor(selv, selv, t1v, op=A.add))
    VS(VV.tensor_tensor(thv, thv, selv, op=A.max))
    gmv = r2(msk[:, 48:64])
    VS(VV.tensor_tensor(gmv, gamv, thv.broadcast_to((128, 4, 4)), op=A.is_ge))
    penv = r2(pen)
    VS(VV.tensor_scalar(pen, msk[:, 48:64], 1.0, BIG, op0=A.subtract, op1=A.mult))
    curv3 = r3(cur)
    for j in range(4):
        VS(VV.tensor_tensor(curv3[:, :, :, j], sbv[:, :, :, j], penv, op=A.add))
    VS(VV.tensor_copy(sb, cur))
    curv = r2(cur, 16)
    mxv = r2(mx, 1)
    trev = r2(tre, 8)
    mskv = r2(msk, 16)

    def rmax16(dst):
        VS(VV.tensor_tensor(trev, curv[:, :, 0:8], curv[:, :, 8:16], op=A.max))
        VS(VV.tensor_tensor(trev[:, :, 0:4], trev[:, :, 0:4], trev[:, :, 4:8],
                            op=A.max))
        VS(VV.tensor_tensor(trev[:, :, 0:2], trev[:, :, 0:2], trev[:, :, 2:4],
                            op=A.max))
        VS(VV.tensor_tensor(dst, trev[:, :, 0:1], trev[:, :, 1:2], op=A.max))

    for rnd in range(3):
        rmax16(mxv)
        VS(VV.tensor_tensor(mskv, curv, mxv.broadcast_to((128, 4, 16)), op=A.is_ge))
        VS(VV.scalar_tensor_tensor(cur, msk, -BIG, cur, op0=A.mult, op1=A.add))
    rmax16(mxv)
    VS(VV.tensor_tensor(mskv, r2(sb, 16), mxv.broadcast_to((128, 4, 16)),
                        op=A.is_ge))
    VS(VV.tensor_tensor(cur, msk, sc32, op=A.mult))
    dnv = r2(dn, 1)
    VS(VV.tensor_tensor(trev, curv[:, :, 0:8], curv[:, :, 8:16], op=A.add))
    VS(VV.tensor_tensor(trev[:, :, 0:4], trev[:, :, 0:4], trev[:, :, 4:8], op=A.add))
    VS(VV.tensor_tensor(trev[:, :, 0:2], trev[:, :, 0:2], trev[:, :, 2:4], op=A.add))
    VS(VV.tensor_tensor(dnv, trev[:, :, 0:1], trev[:, :, 1:2], op=A.add))
    VS(VV.tensor_scalar(dn, dn, 1e-20, None, op0=A.add))
    VS(VV.reciprocal(rcp, dn))
    VS(VV.tensor_tensor(curv, curv, r2(rcp, 1).broadcast_to((128, 4, 16)),
                        op=A.mult))



def _host_layouts(cfg, inputs):
    NSEQ, T, D, H, HD, E, EH, HS = (cfg[k] for k in
                                    ("NSEQ", "T", "D", "H", "HD", "E", "EH", "HS"))
    DC, HP, HHD = D // 128, H // 2, H * HD
    HC = HHD // 128
    MB = HS // 128
    EC = EH // 128
    W_in = np.asarray(inputs["L0_attn_W_in"], np.float32)
    b_in = np.asarray(inputs["L0_attn_b_in"], np.float32)
    W_out = np.asarray(inputs["L0_attn_W_out"], np.float32)
    b_out = np.asarray(inputs["L0_attn_b_out"], np.float32)
    rw = np.asarray(inputs["L0_ffn_router_w"], np.float32)
    cb = np.asarray(inputs["L0_ffn_correction_bias"], np.float32)
    gw = np.asarray(inputs["L0_ffn_gate_w"], np.float32)
    uw = np.asarray(inputs["L0_ffn_up_w"], np.float32)
    dw = np.asarray(inputs["L0_ffn_down_w"], np.float32)
    sg = np.asarray(inputs["L0_ffn_shared_gate_w"], np.float32)
    su = np.asarray(inputs["L0_ffn_shared_up_w"], np.float32)
    sd = np.asarray(inputs["L0_ffn_shared_down_w"], np.float32)

    def lhsT_cols(Wt):
        K, n = Wt.shape
        return np.ascontiguousarray(
            Wt.reshape(K // 128, 128, n).transpose(1, 0, 2).reshape(128, -1))

    w4 = W_in.reshape(D, H, 4, HD)
    sec = {si: np.ascontiguousarray(w4[:, :, si, :].reshape(D, HHD))
           for si in range(4)}
    qkg_parts = []
    for hp in range(HP):
        for si in (0, 1, 3):
            qkg_parts.append(lhsT_cols(sec[si][:, hp * 128:(hp + 1) * 128]))
    wf_qkg = np.concatenate(qkg_parts, axis=1)
    wf_v = lhsT_cols(sec[2])
    wf_out = lhsT_cols(W_out)
    wf_rw = lhsT_cols(rw.T)

    b4 = b_in.reshape(H, 4, HD)
    bsec = {si: np.ascontiguousarray(b4[:, si, :].reshape(HHD)) for si in range(4)}
    col = lambda b: np.ascontiguousarray(b.reshape(-1, 128).T)
    bq_col, bk_col, bg_col = col(bsec[0]), col(bsec[1]), col(bsec[3])
    HPn = bq_col.shape[1]
    bqkg_col = np.zeros((128, HPn * 3), np.float32)
    for hp in range(HPn):
        bqkg_col[:, hp * 3 + 0] = bq_col[:, hp]
        bqkg_col[:, hp * 3 + 1] = bk_col[:, hp]
        bqkg_col[:, hp * 3 + 2] = bg_col[:, hp]
    bv_bc = np.ascontiguousarray(np.broadcast_to(bsec[2], (128, HHD)))
    bo_col = np.ascontiguousarray(b_out.reshape(DC, 128).T)
    cb_bc = np.ascontiguousarray(np.broadcast_to(np.tile(cb, 4), (128, 64)))

    parts = []
    for m in range(MB):
        parts.append(lhsT_cols(np.ascontiguousarray(sg.T[:, m * 128:(m + 1) * 128])))
        parts.append(lhsT_cols(np.ascontiguousarray(su.T[:, m * 128:(m + 1) * 128])))
    for e in range(E):
        parts.append(lhsT_cols(gw[e]))
        parts.append(lhsT_cols(uw[e]))
    sdT = np.ascontiguousarray(sd.T)
    for ct in range(DC):
        parts.append(lhsT_cols(np.ascontiguousarray(sdT[:, ct * 128:(ct + 1) * 128])))
        for e in range(E):
            parts.append(lhsT_cols(np.ascontiguousarray(
                dw[e][:, ct * 128:(ct + 1) * 128])))
    wf_ffn = np.concatenate(parts, axis=1)

    return dict(wf_qkg=wf_qkg, wf_v=wf_v, wf_out=wf_out, wf_rw=wf_rw, wf_ffn=wf_ffn,
                bqkg_col=bqkg_col, bv_bc=bv_bc,
                bo_col=bo_col, cb_bc=cb_bc,
                idm32=np.eye(128, dtype=np.float32))


def kernel(**inputs):
    cfg = CFG_FULL
    NSEQ, T, D = cfg["NSEQ"], cfg["T"], cfg["D"]
    X = np.asarray(inputs["L0_attn_X"], np.float32)
    B = X.shape[0]
    ncores = 8
    assert B == ncores * NSEQ
    host = _host_layouts(cfg, inputs)
    nc = build_program(cfg)
    in_maps = []
    for c in range(ncores):
        xs = X[c * NSEQ:(c + 1) * NSEQ].transpose(0, 2, 1)
        m = dict(host)
        m["x_in"] = np.ascontiguousarray(xs.reshape(NSEQ * D, T))
        in_maps.append(m)
    res = run_bass_kernel_spmd(nc, in_maps, core_ids=list(range(ncores)))
    outs = []
    for c in range(ncores):
        y = res.results[c]["y_out"].reshape(NSEQ, D, T).transpose(0, 2, 1)
        outs.append(y)
    return np.ascontiguousarray(np.concatenate(outs, axis=0))


# version 16
# speedup vs baseline: 1.0003x; 1.0003x over previous
import numpy as np
import ml_dtypes
from contextlib import ExitStack
import bass_rust
import concourse.bass as bass
import concourse.mybir as mybir
from concourse.bass_utils import run_bass_kernel_spmd
from concourse.alu_op_type import AluOpType as ALU

F32 = mybir.dt.float32
BF16 = mybir.dt.bfloat16
FP16 = mybir.dt.float16
AF = mybir.ActivationFunctionType

EPS_A = 1e-4
LN_EPS = 1e-5
BIG = 1e9
PRIME = 1 << 16

CFG_FULL = dict(NSEQ=4, T=2048, D=1024, H=16, HD=64, E=16, EH=256, HS=2048)

CONSTS = [0.0, 1.0, -1.0, 2.0, -2.0, 0.25, 0.5, EPS_A, 2 * EPS_A, EPS_A * EPS_A,
          LN_EPS, 1e-20, BIG, -BIG]


def dsl(base, i, w):
    if isinstance(i, int):
        s0 = base + i * w
        return slice(s0, s0 + w)
    return bass_rust.DynSlice(i * w + base, w)


def dsl2(base, i, step, w):
    if isinstance(i, int):
        s0 = base + i * step
        return slice(s0, s0 + w)
    return bass_rust.DynSlice(i * step + base, w)


class P:

    ENG = ("T", "A", "V", "G", "S")

    def __init__(self, nc, ctx):
        self.nc = nc
        self.ctx = ctx
        self.eng = {"T": nc.tensor, "A": nc.scalar, "V": nc.vector,
                    "G": nc.gpsimd, "S": nc.sync}
        self.sems = {}
        self.cnt = {}
        self.regs = {}
        self.tmp = {e: self.eng[e].alloc_register(f"tmp_{e}") for e in self.ENG}
        self.loops = []

    def sem(self, name, waiters=("T", "A", "V", "G", "S"), prime=True):
        h = self.ctx.enter_context(self.nc.semaphore(name))
        self.sems[name] = h
        for e in waiters:
            r = self.eng[e].alloc_register(f"r_{name}_{e}")
            self.eng[e].reg_mov(r, 0)
            self.regs[(name, e)] = r
        if prime:
            self.eng["S"].sem_inc(h, PRIME)
            self.cnt[name] = PRIME
        else:
            self.cnt[name] = 0
        return h

    def inc(self, instr, name, n=1):
        instr.then_inc(self.sems[name], n)
        self.cnt[name] += n
        return instr

    def wait(self, e, name, lag=0):
        tgt = self.cnt[name] + lag
        eng = self.eng[e]
        r = self.regs[(name, e)]
        eng.reg_add(self.tmp[e], r, tgt)
        eng.wait_ge(self.sems[name], self.tmp[e])

    def loop(self, n):
        prog = self

        class _L:
            def __enter__(sl):
                sl.fori = prog.nc.Fori(0, n)
                i = sl.fori.__enter__()
                prog.loops.append({s: prog.cnt[s] for s in prog.sems})
                return i

            def __exit__(sl, *a):
                entry = prog.loops.pop()
                deltas = {s: prog.cnt[s] - entry[s] for s in prog.sems}
                for (s, e), r in prog.regs.items():
                    if deltas[s]:
                        prog.eng[e].reg_add(r, r, deltas[s])
                sl.fori.__exit__(*a)
                for (s, e), r in prog.regs.items():
                    if deltas[s]:
                        prog.eng[e].reg_add(r, r, -deltas[s] * n)
                for s, d in deltas.items():
                    if d:
                        prog.cnt[s] += d * (n - 1)
        return _L()


def build_program(cfg, use_repl_dma=False, dump_internal=False):
    NSEQ, T, D, H, HD, E, EH, HS = (cfg[k] for k in
                                    ("NSEQ", "T", "D", "H", "HD", "E", "EH", "HS"))
    DC = D // 128
    HP = H // 2
    SC = T // 128
    T4 = T // 512
    TJ = T // 1024
    HHD = H * HD
    HC = HHD // 128
    MB = HS // 128
    EC = EH // 128
    assert E == 16 and H % 2 == 0 and T % 1024 == 0 and D % 128 == 0
    assert HD == 64 and EH % 128 == 0 and HS % 128 == 0 and HHD % 128 == 0

    QKG_COLS = HP * 3 * D
    V_COLS = DC * HHD
    OUT_COLS = HC * D
    RW_COLS = DC * 16
    SEC1 = MB * 2 * D
    EPIECE = 2 * DC * EH
    SEC2 = E * EPIECE
    SEC3C = HS + E * EC * 128
    SEC3 = DC * SEC3C
    FFN_COLS = SEC1 + SEC2 + SEC3

    nc = bass.Bass("TRN2", target_bir_lowering=False, debug=False,
                   detect_race_conditions=False)
    ap = lambda t: t.ap()
    x_in = ap(nc.dram_tensor("x_in", [NSEQ * D, T], F32, kind="ExternalInput"))
    wf_qkg = ap(nc.dram_tensor("wf_qkg", [128, QKG_COLS], F32, kind="ExternalInput"))
    wf_v = ap(nc.dram_tensor("wf_v", [128, V_COLS], F32, kind="ExternalInput"))
    wf_out = ap(nc.dram_tensor("wf_out", [128, OUT_COLS], F32, kind="ExternalInput"))
    wf_rw = ap(nc.dram_tensor("wf_rw", [128, RW_COLS], F32, kind="ExternalInput"))
    wf_ffn = ap(nc.dram_tensor("wf_ffn", [128, FFN_COLS], F32, kind="ExternalInput"))
    bqkg_col = ap(nc.dram_tensor("bqkg_col", [128, HP * 3], F32,
                                 kind="ExternalInput"))
    bv_bc = ap(nc.dram_tensor("bv_bc", [128, HHD], F32, kind="ExternalInput"))
    bo_col = ap(nc.dram_tensor("bo_col", [128, DC], F32, kind="ExternalInput"))
    cb_bc = ap(nc.dram_tensor("cb_bc", [128, 64], F32, kind="ExternalInput"))
    idm32 = ap(nc.dram_tensor("idm32", [128, 128], F32, kind="ExternalInput"))
    y_out = ap(nc.dram_tensor("y_out", [NSEQ * D, T], F32, kind="ExternalOutput"))
    wb_qkg = ap(nc.dram_tensor("wb_qkg", [128, QKG_COLS], BF16))
    wb_v = ap(nc.dram_tensor("wb_v", [128, V_COLS], BF16))
    wb_out = ap(nc.dram_tensor("wb_out", [128, OUT_COLS], BF16))
    wb_rw = ap(nc.dram_tensor("wb_rw", [128, RW_COLS], BF16))
    wb_ffn = ap(nc.dram_tensor("wb_ffn", [128, FFN_COLS], BF16))
    _ik = dict(kind="ExternalOutput") if dump_internal else {}
    h_hbm = ap(nc.dram_tensor("h_hbm", [HHD, T], BF16, **_ik))
    x1_hbm = ap(nc.dram_tensor("x1_hbm", [D, T], F32, **_ik))
    if dump_internal:
        dbg16 = ap(nc.dram_tensor("dbg16", [128, 40960], BF16, kind="ExternalOutput"))
        dbg32 = ap(nc.dram_tensor("dbg32", [128, 1024], F32, kind="ExternalOutput"))

    with ExitStack() as ctx:
        En = ctx.enter_context
        p = P(nc, ctx)
        TT, AA, VV, GG, SS = nc.tensor, nc.scalar, nc.vector, nc.gpsimd, nc.sync

        cp32 = En(nc.sbuf_tensor("cp32", [128, len(CONSTS)], F32))
        cp16 = En(nc.sbuf_tensor("cp16", [128, len(CONSTS)], BF16))
        ones_r16 = En(nc.sbuf_tensor("ones_r16", [128, 128], BF16))
        ones_r32 = En(nc.sbuf_tensor("ones_r32", [128, 128], F32))
        ones_c16 = En(nc.sbuf_tensor("ones_c16", [128, 2], BF16))
        onesw = En(nc.sbuf_tensor("onesw", [128, 2048], BF16))
        idt32 = En(nc.sbuf_tensor("idt32", [128, 128], F32))
        bqkg = En(nc.sbuf_tensor("bqkg", [128, HP * 3], F32))
        bstg = En(nc.sbuf_tensor("bstg", [128, 4], F32))
        bvb = En(nc.sbuf_tensor("bvb", [128, HHD], F32))
        boc = En(nc.sbuf_tensor("boc", [128, DC], F32))
        cbb = En(nc.sbuf_tensor("cbb", [128, 64], F32))
        rww = En(nc.sbuf_tensor("rww", [128, RW_COLS], BF16))
        xbn = En(nc.sbuf_tensor("xbn", [128, DC * T], BF16))

        XF_O = 0
        ST_O = XF_O + DC * 512
        X1_O = ST_O + 2048
        OB_O = X1_O
        RT_O = OB_O + DC * 512
        LG_O = OB_O
        AF32_COLS = max(RT_O + 384, 2 * 4096)
        ar32 = En(nc.sbuf_tensor("ar32", [128, AF32_COLS], F32))

        o = 0
        VAUG_O = o; o += HP * SC * 130
        assert VAUG_O == 0
        QR_O = [o, o + T]; o += 2 * T
        KL_O = [o, o + T]; o += 2 * T
        GH_O = [o, o + T]; o += 2 * T
        PSCR_O = o; o += 3 * T
        SQ_O = o; o += T
        ROW_O = o; o += max(2 * T, DC * 512 - T + 1024)
        RB_O = [o, o + 1024]; o += 2048
        WT_O = [o, o + 1024]; o += 2048
        ZB_OFF = None
        HST_O = o; o += T
        RHO_O = o; o += 1024
        RHOB_O = o; o += 512
        AEV_O = o; o += 512
        WPAIR_O = [o, o]; o += 3 * D
        WBIG_O = o; o += max(V_COLS, OUT_COLS)
        WT_O.append(WBIG_O)
        VSTG_O = [o, o]; o += SC * 130
        ZB_OFF = [SQ_O, ROW_O + 2048, RHO_O]
        att_cols = o
        o = 0
        SH_O = o; o += MB * 512
        assert SH_O == 0
        HW_O = o; o += E * EC * 512
        assert HW_O % 512 == 0
        MSG_O = [o, o + 2 * D]; o += 4 * D
        EB_O = [o, o + EPIECE]; o += 2 * EPIECE
        CBF_O = [o, o + SEC3C]; o += 2 * SEC3C
        BCALL_O = o; o += E * 512
        CMB16_O = o; o += 64
        CMBF_O = o; o += 512
        FT1_O = o; o += EC * 512
        XSTG_O = o; o += DC * 512
        ffn_cols = o
        AB16_COLS = max(att_cols, ffn_cols, 2 * 8192)
        ar16 = En(nc.sbuf_tensor("ar16", [128, AB16_COLS], BF16))

        def a32(start, n):
            return ar32[:, start:start + n]

        def a16(start, n):
            return ar16[:, start:start + n]

        def warm(psbank):
            TT.matmul(psbank[0:1, 0:32], ones_c16[0:1, 0:1], onesw[0:1, 0:32],
                      start=True, stop=True)

        pd2a = En(nc.psum_tensor("pd2a", [128, 1024], F32))
        pd2b = En(nc.psum_tensor("pd2b", [128, 1024], F32))
        pA0 = En(nc.psum_tensor("pA0", [128, 512], F32))
        pA1 = En(nc.psum_tensor("pA1", [128, 512], F32))
        pX = En(nc.psum_tensor("pX", [128, 512], F32))
        pY = En(nc.psum_tensor("pY", [128, 512], F32))

        p.sem("sT", waiters=("A", "V", "G", "S"))
        p.sem("sA", waiters=("T", "A", "V", "G", "S"))
        p.sem("sV", waiters=("T", "A", "V", "G", "S"))
        p.sem("dInit", waiters=("T", "V"), prime=False)
        p.sem("dCf0", waiters=("V",), prime=False); p.sem("dCf1", waiters=("V",), prime=False)
        p.sem("dCb0", waiters=("V", "S"), prime=False); p.sem("dCb1", waiters=("V", "S"), prime=False)
        p.sem("dXf", waiters=("V",), prime=False)
        p.sem("dHb", waiters=("T",), prime=False)
        p.sem("dWbig", waiters=("T",), prime=False)
        p.sem("dWp0", waiters=("T",), prime=False); p.sem("dWp1", waiters=("T",), prime=False)
        p.sem("dMsg0", waiters=("T",), prime=False); p.sem("dMsg1", waiters=("T",), prime=False)
        p.sem("dEb0", waiters=("T",), prime=False); p.sem("dEb1", waiters=("T",), prime=False)
        p.sem("dCb20", waiters=("T",), prime=False); p.sem("dCb21", waiters=("T",), prime=False)
        p.sem("dShift", waiters=("T", "V"), prime=False)
        p.sem("dRho", waiters=("T", "V"), prime=False)
        p.sem("dRow", waiters=("T",), prime=False)
        p.sem("dH", waiters=("S",), prime=False)
        p.sem("dX1", waiters=("S", "V"), prime=False)
        p.sem("dY", waiters=("G", "V", "A"), prime=False)
        p.sem("dVstg", waiters=("T",), prime=False)

        def dmaS(sem, out, in_, **kw):
            p.inc(SS.dma_start(out=out, in_=in_, **kw), sem, 16)

        def dmaG(sem, out, in_, **kw):
            p.inc(GG.dma_start(out=out, in_=in_, **kw), sem, 16)

        C32 = {v: cp32[:, i:i + 1] for i, v in enumerate(CONSTS)}
        for i, v in enumerate(CONSTS):
            nc.const_aps.register(F32, v, C32[v])
            nc.const_aps.register(BF16, v, cp16[:, i:i + 1])

        for i, v in enumerate(CONSTS):
            VV.memset(cp32[:, i:i + 1], v)
            VV.memset(cp16[:, i:i + 1], v)
        VV.memset(ones_r16[0:1, :], 1.0)
        VV.memset(ones_r16[64:65, :], 1.0)
        VV.memset(ones_r32[0:1, :], 1.0)
        VV.memset(ones_r32[64:65, :], 1.0)
        VV.memset(ones_c16[:, :], 1.0)
        VV.memset(ar16[:, att_cols:att_cols + 64], 0.0)
        p.inc(VV.memset(onesw[:, :], 1.0), "sV")
        dmaS("dInit", idt32[:], idm32)
        dmaS("dInit", bqkg[:], bqkg_col); dmaS("dInit", bvb[:], bv_bc)
        dmaS("dInit", boc[:], bo_col); dmaS("dInit", cbb[:], cb_bc)

        CH = 4096
        cast_jobs = []
        for src, dst, cols in ((wf_qkg, wb_qkg, QKG_COLS), (wf_v, wb_v, V_COLS),
                               (wf_out, wb_out, OUT_COLS), (wf_rw, wb_rw, RW_COLS),
                               (wf_ffn, wb_ffn, FFN_COLS)):
            c0 = 0
            while c0 < cols:
                w = min(CH, cols - c0)
                cast_jobs.append((src, dst, c0, w))
                c0 += w
        for j, (src, dst, c0, w) in enumerate(cast_jobs):
            par = j % 2
            fbuf = a32(par * 4096, w)
            bbuf = a16(par * 4096, w)
            if par == 0:
                if j >= 2:
                    p.wait("S", "sV", -1)
                dmaS(f"dCf{par}", fbuf, src[:, c0:c0 + w])
            else:
                if j >= 2:
                    p.wait("A", "sV", -1)
                p.inc(AA.dma_start(out=fbuf, in_=src[:, c0:c0 + w]),
                      f"dCf{par}", 16)
            p.wait("V", f"dCf{par}")
            if j >= 2:
                p.wait("V", f"dCb{par}")
            p.inc(VV.tensor_copy(bbuf, fbuf), "sV")
            p.wait("G", "sV")
            dmaG(f"dCb{par}", dst[:, c0:c0 + w], bbuf)
        p.wait("S", "dCb0")
        p.wait("S", "dCb1")
        dmaS("dInit", rww[:], wb_rw)
        p.wait("T", "dInit")
        p.wait("V", "dInit")
        nc.all_engine_barrier()

        def xsl(k, tt, width=512):
            return xbn[:, dsl(k * T, tt, width)] if width == 512 else \
                xbn[:, k * T + tt * 1024: k * T + (tt + 1) * 1024]

        def ln_phase(src_tile):
            XO = (XF_O, X1_O)
            LSEM = ("dXf", "dRho")

            def ld(tt):
                p.wait("S", "sV")
                if tt <= 1:
                    p.wait("S", "dX1")
                dmaS(LSEM[tt % 2],
                     a32(XO[tt % 2], DC * 512).rearrange("q (k w) -> q k w",
                                                         k=DC), src_tile(tt))
            ld(0)
            for tt in range(T4):
                xf = a32(XO[tt % 2], DC * 512)
                p.wait("V", LSEM[tt % 2])
                for k in range(DC):
                    ins = VV.tensor_copy(xsl(k, tt), xf[:, k * 512:(k + 1) * 512])
                p.inc(ins, "sV")
                if tt + 1 < T4:
                    ld(tt + 1)
                sqb = a16(SQ_O, DC * 512)
                for k in range(DC):
                    ins = VV.tensor_tensor(sqb[:, k * 512:(k + 1) * 512],
                                           xsl(k, tt), xsl(k, tt), op=ALU.mult)
                p.inc(ins, "sV")
                p.wait("T", "sV")
                for k in range(DC):
                    TT.matmul(pX[0:1, :], ones_c16[:, 0:1], xsl(k, tt),
                              start=(k == 0), stop=(k == DC - 1))
                for k in range(DC):
                    ins = TT.matmul(pY[0:1, :], ones_c16[:, 0:1],
                                    sqb[:, k * 512:(k + 1) * 512],
                                    start=(k == 0), stop=(k == DC - 1))
                p.inc(ins, "sT")
                st = a32(ST_O, 2048)
                p.wait("V", "sT")
                VV.tensor_scalar(st[0:1, 0:512], pX[0:1, :], 1.0 / D, None,
                                 op0=ALU.mult)
                VV.tensor_tensor(st[0:1, 512:1024], st[0:1, 0:512], st[0:1, 0:512],
                                 op=ALU.mult)
                p.inc(VV.scalar_tensor_tensor(st[0:1, 1024:1536], pY[0:1, :],
                                              1.0 / D, st[0:1, 512:1024],
                                              op0=ALU.mult, op1=ALU.subtract), "sV")
                p.wait("A", "sV")
                p.inc(AA.activation(st[0:1, 1536:2048], st[0:1, 1024:1536], AF.Ln,
                                    bias=LN_EPS, scale=1.0), "sA")
                p.wait("A", "sA", -1)
                p.inc(AA.activation(st[0:1, 1024:1536], st[0:1, 1536:2048], AF.Exp,
                                    scale=-0.5), "sA")
                p.wait("V", "sA")
                mrow = a16(SQ_O + DC * 512, 512)
                p.inc(VV.tensor_copy(mrow[0:1, :], st[0:1, 0:512]), "sV")
                p.wait("T", "sV")
                TT.matmul(pX[:, :], ones_r16[0:1, :], mrow[0:1, :],
                          start=True, stop=True)
                p.inc(TT.matmul(pY[:, :], ones_r32[0:1, :], st[0:1, 1024:1536],
                                start=True, stop=True), "sT")
                p.wait("A", "sT")
                mbb = a16(SQ_O + DC * 512 + 512, 512)
                p.inc(AA.activation(mbb[:, :], pX[:, :], AF.Copy), "sA")
                p.wait("V", "sA")
                for k in range(DC):
                    VV.tensor_tensor(xsl(k, tt), xsl(k, tt), mbb[:, :],
                                     op=ALU.subtract)
                    ins = VV.tensor_tensor(xsl(k, tt), xsl(k, tt), pY[:, :],
                                           op=ALU.mult)
                p.inc(ins, "sV")
            nc.all_engine_barrier()

        def emit_A(sc, tj, vbase, u, vlag=0):
            wt = a16(WT_O[sc % 3], 1024)
            p.wait("T", "sV", vlag)
            lh = ar16[:, vbase + sc * 130 + u * 65:
                      vbase + sc * 130 + u * 65 + 128]
            TT.matmul(pA0[0:128, :], lh, wt[:, 0:512],
                      start=(sc == 0), stop=(sc == SC - 1))
            p.inc(TT.matmul(pA1[0:128, :], lh, wt[:, 512:1024],
                            start=(sc == 0), stop=(sc == SC - 1)), "sT")

        with p.loop(NSEQ) as s:
            nc.all_engine_barrier()
            x4 = x_in.rearrange("(sq k q) t -> sq q k t", q=128, k=DC)
            ln_phase(lambda tt: x4[dsl(0, s, 1), :, :, tt * 512:(tt + 1) * 512])

            for bo in (QR_O[0], QR_O[1], KL_O[0], KL_O[1]):
                ins = VV.memset(ar16[64:128, bo:bo + T], 0.0)
            p.inc(ins, "sV")

            wv = a16(WBIG_O, V_COLS)
            p.wait("S", "sT")
            dmaS("dWbig", wv, wb_v)
            vaug_full = a16(VAUG_O, HP * SC * 130)
            vview = vaug_full.rearrange("q (hp sc u d) -> q hp sc u d",
                                        hp=HP, sc=SC, u=2)
            p.inc(VV.memset(vview[:, :, :, :, 64:65], 1.0), "sV")
            p.wait("T", "dWbig")
            p.wait("T", "sV")
            NHALF = HHD // 512 if HHD >= 512 else 1
            HW_HALF = min(HHD, 512)
            grp_i = 0
            for st_ in range(SC):
                for half in range(NHALF):
                    ps = pX if grp_i % 2 == 0 else pY
                    if grp_i >= 2:
                        p.wait("T", "sV", -1)
                    for k in range(DC):
                        ins = TT.matmul(ps[:, 0:HW_HALF],
                                        xbn[:, k * T + st_ * 128:k * T + st_ * 128 + 128],
                                        wv[:, k * HHD + half * 512:
                                           k * HHD + half * 512 + HW_HALF],
                                        start=(k == 0), stop=(k == DC - 1))
                    p.inc(ins, "sT")
                    p.wait("V", "sT")
                    nhp = HW_HALF // 128
                    hp0 = half * 4
                    outv = vview[:, hp0:hp0 + nhp, st_, :, 0:64]
                    inv = ps[:, 0:HW_HALF].rearrange("q (hp u d) -> q hp u d",
                                                     hp=nhp, u=2)
                    bvv = bvb[:, half * 512:half * 512 + HW_HALF].rearrange(
                        "q (hp u d) -> q hp u d", hp=nhp, u=2)
                    p.inc(VV.tensor_tensor(outv, inv, bvv, op=ALU.add), "sV")
                    grp_i += 1
            nc.all_engine_barrier()

            with p.loop(HP // 2) as hp2:
                for u2 in range(2):
                    hp = hp2 * 2 + u2
                    wp = a16(WPAIR_O[u2], 3 * D)
                    p.wait("S", "sT", 0)
                    dmaS(f"dWp{u2}", wp, wb_qkg[:, dsl(0, hp, 3 * D)])
                    vstg = a16(VSTG_O[u2], SC * 130)
                    p.wait("V", "sT")
                    p.inc(VV.tensor_copy(bstg[:, 0:3], bqkg[:, dsl(0, hp, 3)]), "sV")
                    p.inc(VV.tensor_copy(vstg, ar16[:, dsl(0, hp, SC * 130)]), "sV")
                    p.wait("V", "sV")
                    p.wait("T", f"dWp{u2}")
                    gi = 0
                    for si, (dst0, scl) in enumerate(
                            ((QR_O, -2.0), (KL_O, 1.0), (GH_O, 1.0))):
                        for tt in range(T4):
                            ps = pX if gi % 2 == 0 else pY
                            p.wait("T", "sV", -1 if gi >= 2 else -2)
                            for k in range(DC):
                                ins = TT.matmul(
                                    ps[:, :],
                                    wp[:, si * D + k * 128:si * D + (k + 1) * 128],
                                    xsl(k, tt), start=(k == 0), stop=(k == DC - 1))
                            p.inc(ins, "sT")
                            p.wait("V", "sT")
                            bcl = bstg[:, si:si + 1]
                            d0 = ar16[0:64, dst0[0] + tt * 512:dst0[0] + (tt + 1) * 512]
                            d1 = ar16[64:128, PSCR_O + si * T + tt * 512:
                                      PSCR_O + si * T + (tt + 1) * 512]
                            if scl == 1.0:
                                VV.tensor_scalar(d0, ps[0:64, :], bcl[0:64, :], None,
                                                 op0=ALU.add)
                                p.inc(VV.tensor_scalar(d1, ps[64:128, :],
                                                       bcl[64:128, :], None,
                                                       op0=ALU.add), "sV")
                            else:
                                VV.tensor_scalar(d0, ps[0:64, :], bcl[0:64, :], scl,
                                                 op0=ALU.add, op1=ALU.mult)
                                p.inc(VV.tensor_scalar(d1, ps[64:128, :],
                                                       bcl[64:128, :], scl,
                                                       op0=ALU.add, op1=ALU.mult),
                                      "sV")
                            gi += 1
                    p.wait("G", "sV")
                    dmaG("dShift", ar16[0:64, QR_O[1]:QR_O[1] + T],
                         ar16[64:128, PSCR_O:PSCR_O + T])
                    dmaG("dShift", ar16[0:64, KL_O[1]:KL_O[1] + T],
                         ar16[64:128, PSCR_O + T:PSCR_O + 2 * T])
                    dmaG("dShift", ar16[0:64, GH_O[1]:GH_O[1] + T],
                         ar16[64:128, PSCR_O + 2 * T:PSCR_O + 3 * T])
                    for uu in range(2):
                        dmaG("dShift", ar16[66:68, QR_O[uu]:QR_O[uu] + T],
                             onesw[0:2, 0:T])
                        dmaG("dShift", ar16[64:66, KL_O[uu]:KL_O[uu] + T],
                             onesw[0:2, 0:T])

                    for u in range(2):
                        qr = ar16[:, QR_O[u]:QR_O[u] + T]
                        kl = ar16[:, KL_O[u]:KL_O[u] + T]
                        gh = ar16[:, GH_O[u]:GH_O[u] + T]
                        if u == 1:
                            p.wait("V", "dShift")
                        sqs = a16(SQ_O, T)
                        rows = a16(ROW_O, 2 * T)
                        for (tile, qscale, hoff) in ((qr, 0.25, 0), (kl, 1.0, 1)):
                            p.inc(VV.tensor_tensor(sqs[0:64, :], tile[0:64, :],
                                                   tile[0:64, :], op=ALU.mult), "sV")
                            p.wait("T", "sV")
                            for tt in range(T4):
                                pdx = pd2a if tt < 2 else pd2b
                                col = (tt % 2) * 512
                                ins = TT.matmul(pdx[0:1, col:col + 512],
                                                ones_c16[0:64, 0:1],
                                                sqs[0:64, tt * 512:(tt + 1) * 512],
                                                start=True, stop=True)
                            p.inc(ins, "sT")
                            p.wait("A", "sT")
                            W1 = min(T4, 2) * 512
                            ins = AA.activation(rows[0:1, 0:W1], pd2a[0:1, 0:W1],
                                                AF.Copy, scale=qscale)
                            if T4 > 2:
                                ins = AA.activation(rows[0:1, 1024:2048],
                                                    pd2b[0:1, :], AF.Copy,
                                                    scale=qscale)
                            p.inc(ins, "sA")
                            p.wait("V", "sA")
                            ins = VV.scalar_tensor_tensor(rows[0:1, T:T + W1],
                                                          pd2a[0:1, 0:W1], qscale,
                                                          rows[0:1, 0:W1],
                                                          op0=ALU.mult,
                                                          op1=ALU.subtract)
                            if T4 > 2:
                                ins = VV.scalar_tensor_tensor(
                                    rows[0:1, T + 1024:T + 2048], pd2b[0:1, :],
                                    qscale, rows[0:1, 1024:2048],
                                    op0=ALU.mult, op1=ALU.subtract)
                            p.inc(ins, "sV")
                            p.wait("G", "sV")
                            if hoff == 0:
                                dmaG("dShift", qr[64:65, :], rows[0:1, 0:T])
                                dmaG("dShift", qr[65:66, :], rows[0:1, T:2 * T])
                            else:
                                dmaG("dShift", kl[66:67, :], rows[0:1, 0:T])
                                dmaG("dShift", kl[67:68, :], rows[0:1, T:2 * T])
                        p.wait("T", "dShift")

                        vau = a16(VSTG_O[u2], SC * 130)
                        if u == 0:
                            p.wait("T", "sV")
                        pend_rho = [None]
                        for tj in range(TJ):
                            def stage_T(sc):
                                par = sc % 2
                                pd = pd2a if par == 0 else pd2b
                                if sc == 0:
                                    p.wait("V", "dShift")
                                p.wait("T", "sA", 0 if sc <= 1 else -2)
                                if sc <= 1:
                                    p.wait("T", "sV")
                                TT.matmul(pd[:, 0:512],
                                          kl[0:128, sc * 128:(sc + 1) * 128],
                                          qr[0:128, tj * 1024:tj * 1024 + 512],
                                          start=True, stop=True)
                                p.inc(TT.matmul(
                                    pd[:, 512:1024], kl[0:128, sc * 128:(sc + 1) * 128],
                                    qr[0:128, tj * 1024 + 512:tj * 1024 + 1024],
                                    start=True, stop=True), "sT")

                            def stage_ACT(sc, skew):
                                if sc < SC:
                                    par = sc % 2
                                    pd = pd2a if par == 0 else pd2b
                                    lb = ar16[:, RB_O[par]:RB_O[par] + 1024]\
                                        .bitcast(FP16)[:, 0:1024]
                                    p.wait("A", "sT", -1 if skew else 0)
                                    p.inc(AA.activation(lb, pd[:, :], AF.Ln,
                                                        bias=EPS_A * EPS_A,
                                                        scale=1.0), "sA")
                                if sc >= 1:
                                    parp = (sc - 1) % 2
                                    lbp = ar16[:, RB_O[parp]:RB_O[parp] + 1024]\
                                        .bitcast(FP16)[:, 0:1024]
                                    zb = a16(ZB_OFF[(sc - 1) % 3], 1024)
                                    p.inc(AA.activation(zb, lbp, AF.Exp,
                                                        scale=-0.5), "sA")

                            def stage_V(sc, tlag=-2):
                                par = sc % 2
                                zb = a16(ZB_OFF[sc % 3], 1024)
                                bp = a16(ROW_O + par * 1024, 1024)
                                wt = a16(WT_O[sc % 3], 1024)
                                p.wait("V", "sA")
                                p.wait("V", "sT", tlag)
                                with nc.allow_low_precision(reason="bf16 attn wts"):
                                    VV.tensor_scalar(bp, zb, -EPS_A, 1.0,
                                                     op0=ALU.mult, op1=ALU.add)
                                    VV.tensor_tensor(bp, zb, bp, op=ALU.mult)
                                    p.inc(VV.tensor_tensor(wt, bp, bp, op=ALU.mult),
                                          "sV")

                            for sc in range(SC):
                                stage_T(sc)
                                if sc >= 3:
                                    emit_A(sc - 3, tj, VSTG_O[u2], u, vlag=-1)
                                stage_ACT(sc, skew=(sc >= 3))
                                if sc >= 1:
                                    stage_V(sc - 1)
                                if sc == 1 and pend_rho[0] is not None:
                                    pend_rho[0]()
                                    pend_rho[0] = None
                            p.wait("A", "sV", -1)
                            stage_ACT(SC, skew=True)
                            stage_V(SC - 1, tlag=0)
                            emit_A(SC - 3, tj, VSTG_O[u2], u, vlag=-2)
                            emit_A(SC - 2, tj, VSTG_O[u2], u, vlag=-1)
                            emit_A(SC - 1, tj, VSTG_O[u2], u)
                            rho0 = a16(RHO_O, 1024)
                            rho1 = a16(RHOB_O, 1024)
                            p.wait("A", "sT")
                            p.inc(AA.activation(rho0[64:65, 0:512], pA0[64:65, :],
                                                AF.Ln, bias=EPS_A, scale=1.0), "sA")
                            p.inc(AA.activation(rho1[64:65, 0:512], pA1[64:65, :],
                                                AF.Ln, bias=EPS_A, scale=1.0), "sA")
                            p.wait("A", "sA", -1)
                            p.inc(AA.activation(rho0[64:65, 512:1024],
                                                rho0[64:65, 0:512],
                                                AF.Exp, scale=-1.0), "sA")
                            p.wait("A", "sA", -1)
                            p.inc(AA.activation(rho1[64:65, 512:1024],
                                                rho1[64:65, 0:512],
                                                AF.Exp, scale=-1.0), "sA")
                            exp1_mark = p.cnt["sA"]

                            def mk_rho_tail(tj, rho0, rho1, exp1_mark):
                                def tail():
                                    p.wait("T", "sV")
                                    p.wait("T", "sA",
                                           exp1_mark - 1 - p.cnt["sA"])
                                    p.inc(TT.matmul(pX[0:64, :],
                                                    ones_r16[64:65, 0:64],
                                                    rho0[64:65, 512:1024],
                                                    start=True, stop=True), "sT")
                                    p.wait("T", "sA",
                                           exp1_mark - p.cnt["sA"])
                                    p.inc(TT.matmul(pY[0:64, :],
                                                    ones_r16[64:65, 0:64],
                                                    rho1[64:65, 512:1024],
                                                    start=True, stop=True), "sT")
                                    hs = lambda v: ar16[0:64,
                                                        HST_O + (tj * 2 + v) * 512:
                                                        HST_O + (tj * 2 + v + 1) * 512]
                                    ghs = lambda v: gh[0:64, tj * 1024 + v * 512:
                                                       tj * 1024 + (v + 1) * 512]
                                    p.wait("V", "sT", -2)
                                    VV.tensor_tensor(hs(0), pA0[0:64, :], ghs(0),
                                                     op=ALU.mult)
                                    p.wait("V", "sT", -1)
                                    VV.tensor_tensor(hs(0), hs(0), pX[0:64, :],
                                                     op=ALU.mult)
                                    VV.tensor_tensor(hs(1), pA1[0:64, :], ghs(1),
                                                     op=ALU.mult)
                                    p.wait("V", "sT")
                                    p.inc(VV.tensor_tensor(hs(1), hs(1),
                                                           pY[0:64, :],
                                                           op=ALU.mult), "sV")
                                return tail
                            tl = mk_rho_tail(tj, rho0, rho1, exp1_mark)
                            if tj < TJ - 1:
                                pend_rho[0] = tl
                            else:
                                tl()
                        p.wait("G", "sV")
                        dmaG("dH", h_hbm[dsl(0, hp * 2 + u, HD), :],
                             ar16[0:64, HST_O:HST_O + T])
            nc.all_engine_barrier()

            wo = a16(WBIG_O, OUT_COLS)
            p.wait("S", "dH")
            p.wait("S", "sT")
            dmaS("dWbig", wo, wb_out)
            hv = h_hbm.rearrange("(k q) t -> q k t", q=128)
            x1v = x1_hbm.rearrange("(k q) t -> q k t", q=128)
            for tt in range(T4):
                xf = a32(XF_O, DC * 512)
                p.wait("S", "sV")
                dmaS("dXf", xf.rearrange("q (k w) -> q k w", k=DC),
                     x4[dsl(0, s, 1), :, :, tt * 512:(tt + 1) * 512])
                hbuf = a16(QR_O[0], HC * 512)
                p.wait("S", "sT")
                dmaS("dHb", hbuf.rearrange("q (k w) -> q k w", k=HC),
                     hv[:, :, tt * 512:(tt + 1) * 512])
                p.wait("T", "dWbig")
                p.wait("T", "dHb")
                x1sl = a32(X1_O, DC * 512)
                p.wait("V", "dX1")
                for ct in range(DC):
                    ps = pX if ct % 2 == 0 else pY
                    if ct >= 2:
                        p.wait("T", "sV", -1)
                    for k in range(HC):
                        ins = TT.matmul(ps[:, :],
                                        wo[:, k * D + ct * 128:k * D + (ct + 1) * 128],
                                        hbuf[:, k * 512:(k + 1) * 512],
                                        start=(k == 0), stop=(k == HC - 1))
                    p.inc(ins, "sT")
                    p.wait("V", "sT")
                    if ct == 0:
                        p.wait("V", "dXf")
                    p.inc(VV.scalar_tensor_tensor(
                        x1sl[:, ct * 512:(ct + 1) * 512], ps[:, :], boc[:, ct:ct + 1],
                        xf[:, ct * 512:(ct + 1) * 512], op0=ALU.add, op1=ALU.add),
                        "sV")
                p.wait("G", "sV")
                dmaG("dX1", x1v[:, :, tt * 512:(tt + 1) * 512],
                     x1sl.rearrange("q (k w) -> q k w", k=DC))
            nc.all_engine_barrier()

            p.wait("S", "dX1")
            ln_phase(lambda tt: x1v[:, :, tt * 512:(tt + 1) * 512])

            with p.loop(T4) as tt:
                xf = a32(XF_O, DC * 512)
                p.wait("S", "sV")
                dmaS("dXf", xf.rearrange("q (k w) -> q k w", k=DC),
                     x1v[:, :, dsl(0, tt, 512)])
                xstg = a16(XSTG_O, DC * 512)
                p.wait("V", "sT")
                xb4 = xbn[:].rearrange("q (k t) -> q k t", k=DC)
                p.inc(VV.tensor_copy(xstg.rearrange("q (k w) -> q k w", k=DC),
                                     xb4[:, :, dsl(0, tt, 512)]), "sV")

                def xst(k):
                    return xstg[:, k * 512:(k + 1) * 512]
                p.wait("T", "sA")
                p.wait("T", "sV")
                for k in range(DC):
                    ins = TT.matmul(pX[0:16, :], rww[:, k * 16:(k + 1) * 16],
                                    xst(k), start=(k == 0), stop=(k == DC - 1))
                p.inc(ins, "sT")
                lgf = a32(LG_O, 512)
                p.wait("A", "sT")
                p.wait("A", "dY")
                p.inc(AA.activation(lgf[0:16, :], pX[0:16, :], AF.Copy), "sA")
                rt = a32(RT_O, 384)
                p.wait("T", "sA")
                for tok in range(4):
                    if tok > 0:
                        p.wait("T", "sA")
                    p.inc(TT.transpose(pY[0:128, 0:16],
                                       lgf[0:16, tok * 128:(tok + 1) * 128],
                                       idt32[0:16, 0:16]), "sT")
                    p.wait("A", "sT")
                    p.inc(AA.activation(rt[:, tok * 16:(tok + 1) * 16],
                                        pY[0:128, 0:16], AF.Sigmoid), "sA")
                p.wait("V", "sA")
                _routing(p, VV, nc, rt, cbb[:, :])
                cmbf = a16(CMBF_O, 512)
                p.wait("T", "sV")
                for tok in range(4):
                    if tok > 0:
                        p.wait("T", "sA")
                    p.inc(TT.transpose(pY[0:16, 0:128],
                                       rt[:, 128 + tok * 16:128 + (tok + 1) * 16],
                                       idt32[:, :]), "sT")
                    p.wait("A", "sT")
                    p.inc(AA.activation(cmbf[0:16, tok * 128:(tok + 1) * 128],
                                        pY[0:16, 0:128], AF.Copy), "sA")
                bcall = a16(BCALL_O, E * 512)
                p.wait("G", "sA")
                if use_repl_dma:
                    for e in range(E):
                        dmaG("dRow", bcall[:, e * 512:(e + 1) * 512],
                             cmbf[e:e + 1, :].broadcast_to((128, 512)))
                else:
                    p.wait("G", "sA")
                    for e in range(E):
                        dmaG("dRow", bcall[64:65, e * 512:(e + 1) * 512],
                             cmbf[e:e + 1, :])
                    p.wait("T", "dRow")
                    for e in range(E):
                        if e > 0:
                            p.wait("T", "sA")
                        p.inc(TT.matmul(pY[:, :], ones_r16[64:65, :],
                                        bcall[64:65, e * 512:(e + 1) * 512],
                                        start=True, stop=True), "sT")
                        p.wait("A", "sT")
                        p.inc(AA.activation(bcall[:, e * 512:(e + 1) * 512],
                                            pY[:, :], AF.Copy), "sA")
                for m in range(MB):
                    pm = m % 2
                    mo = a16(MSG_O[pm], 2 * D)
                    if m >= 2:
                        p.wait("S", "sT", -2)
                    dmaS(f"dMsg{pm}", mo, wb_ffn[:, m * 2 * D:(m + 1) * 2 * D])
                    p.wait("T", f"dMsg{pm}")
                    p.wait("T", "sA")
                    for k in range(DC):
                        ins = TT.matmul(pX[:, :], mo[:, k * 128:(k + 1) * 128],
                                        xst(k), start=(k == 0),
                                        stop=(k == DC - 1))
                    p.inc(ins, "sT")
                    p.wait("T", "sV")
                    for k in range(DC):
                        ins = TT.matmul(pY[:, :],
                                        mo[:, D + k * 128:D + (k + 1) * 128],
                                        xst(k), start=(k == 0),
                                        stop=(k == DC - 1))
                    p.inc(ins, "sT")
                    ft1 = a16(FT1_O, 512)
                    p.wait("A", "sT", -1)
                    p.wait("A", "sV")
                    p.inc(AA.activation(ft1, pX[:, :], AF.Silu), "sA")
                    p.wait("V", "sA")
                    p.wait("V", "sT")
                    p.inc(VV.tensor_tensor(ar16[:, SH_O + m * 512:SH_O + (m + 1) * 512],
                                           ft1, pY[:, :], op=ALU.mult), "sV")
                for e in range(E):
                    pe = e % 2
                    eb = a16(EB_O[pe], EPIECE)
                    if e >= 2:
                        p.wait("S", "sT", -2)
                    dmaS(f"dEb{pe}", eb, wb_ffn[:, SEC1 + e * EPIECE:SEC1 + (e + 1) * EPIECE])
                    p.wait("T", f"dEb{pe}")
                    if pe == 0:
                        gb = (pd2a[:, 0:512], pd2a[:, 512:1024])
                        ub = (pd2b[:, 0:512], pd2b[:, 512:1024])
                        gsil = (pd2a[:, 0:EC * 512],)
                    else:
                        gb = (pA0[:, :], pA1[:, :])
                        ub = (pX[:, :], pY[:, :])
                        gsil = (pA0[:, :], pA1[:, :])[:EC]
                    if e >= 2:
                        p.wait("T", "sA", -1)
                    for mt in range(EC):
                        for k in range(DC):
                            ins = TT.matmul(
                                gb[mt], eb[:, k * EH + mt * 128:k * EH + (mt + 1) * 128],
                                xst(k), start=(k == 0), stop=(k == DC - 1))
                    p.inc(ins, "sT")
                    if e >= 2:
                        p.wait("T", "sV", -1)
                    for mt in range(EC):
                        for k in range(DC):
                            ins = TT.matmul(
                                ub[mt], eb[:, DC * EH + k * EH + mt * 128:
                                           DC * EH + k * EH + (mt + 1) * 128],
                                xst(k), start=(k == 0), stop=(k == DC - 1))
                    p.inc(ins, "sT")
                    ft1 = a16(FT1_O, EC * 512)
                    p.wait("A", "sT", -1)
                    p.wait("A", "sV")
                    if pe == 0:
                        p.inc(AA.activation(ft1, gsil[0], AF.Silu), "sA")
                    else:
                        for gi, gpc in enumerate(gsil):
                            ins = AA.activation(ft1[:, gi * 512:(gi + 1) * 512],
                                                gpc, AF.Silu)
                        p.inc(ins, "sA")
                    p.wait("V", "sA")
                    p.wait("V", "sT")
                    for mt in range(EC):
                        VV.tensor_tensor(ft1[:, mt * 512:(mt + 1) * 512],
                                         ft1[:, mt * 512:(mt + 1) * 512],
                                         ub[mt], op=ALU.mult)
                    for mt in range(EC):
                        ins = VV.tensor_tensor(
                            ar16[:, HW_O + (e * EC + mt) * 512:
                                 HW_O + (e * EC + mt + 1) * 512],
                            ft1[:, mt * 512:(mt + 1) * 512],
                            ar16[:, BCALL_O + e * 512:BCALL_O + (e + 1) * 512],
                            op=ALU.mult)
                    p.inc(ins, "sV")
                for ct in range(DC):
                    pc = ct % 2
                    cb = a16(CBF_O[pc], SEC3C)
                    if ct >= 2:
                        p.wait("S", "sT", -1)
                    dmaS(f"dCb2{pc}", cb, wb_ffn[:, SEC1 + SEC2 + ct * SEC3C:
                                                 SEC1 + SEC2 + (ct + 1) * SEC3C])
                    p.wait("T", f"dCb2{pc}")
                    pA = pA0 if pc == 0 else pA1
                    if ct < 2:
                        p.wait("T", "sV")
                    else:
                        p.wait("T", "sV", -1)
                    for m in range(MB):
                        TT.matmul(pA[:, :], cb[:, m * 128:(m + 1) * 128],
                                  ar16[:, SH_O + m * 512:SH_O + (m + 1) * 512],
                                  start=(m == 0), stop=False)
                    for e in range(E):
                        for mt in range(EC):
                            last = (e == E - 1 and mt == EC - 1)
                            ins = TT.matmul(
                                pA[:, :],
                                cb[:, HS + (e * EC + mt) * 128:
                                   HS + (e * EC + mt + 1) * 128],
                                ar16[:, HW_O + (e * EC + mt) * 512:
                                     HW_O + (e * EC + mt + 1) * 512],
                                start=False, stop=last)
                    p.inc(ins, "sT")
                    p.wait("V", "sT")
                    if ct == 0:
                        p.wait("V", "dXf")
                        p.wait("V", "dY")
                    p.inc(VV.tensor_tensor(
                        ar32[:, OB_O + ct * 512:OB_O + (ct + 1) * 512], pA[:, :],
                        ar32[:, XF_O + ct * 512:XF_O + (ct + 1) * 512],
                        op=ALU.add), "sV")
                p.wait("G", "sV")
                yv = y_out.rearrange("(sq k q) t -> sq q k t", q=128, k=DC)
                ob = a32(OB_O, DC * 512)
                dmaG("dY", yv[dsl(0, s, 1), :, :, dsl(0, tt, 512)],
                     ob.rearrange("q (k w) -> q k w", k=DC))
            nc.all_engine_barrier()

        if dump_internal:
            nd = 0
            for off, n in ((SH_O, MB * 512), (HW_O, E * EC * 512),
                           (BCALL_O, E * 512), (XSTG_O, DC * 512)):
                dmaG("dY", dbg16[:, nd:nd + n], ar16[:, off:off + n])
                nd += n
            dmaG("dY", dbg32[:, 0:384], ar32[:, RT_O:RT_O + 384])
            dmaG("dY", dbg32[:, 384:384 + 512], ar32[:, LG_O:LG_O + 512])
        p.wait("G", "dY")
    return nc


def _routing(p, VV, nc, rt, cbb):
    from concourse.alu_op_type import AluOpType as A

    def VS(ins):
        p.inc(ins, "sV")
        p.wait("V", "sV")
        return ins

    sc32 = rt[:, 0:64]
    sb = rt[:, 64:128]
    cur = rt[:, 128:192]
    gam = rt[:, 192:208]
    tA = rt[:, 208:212]; tB = rt[:, 212:216]
    ta = rt[:, 216:220]; tb = rt[:, 220:224]
    th = rt[:, 224:228]
    ge = rt[:, 228:232]; sel = rt[:, 232:236]; tmp1 = rt[:, 236:240]
    pen = rt[:, 240:256]
    mx = rt[:, 256:260]; dn = rt[:, 260:264]; rcp = rt[:, 264:268]
    msk = rt[:, 272:336]
    tre = rt[:, 336:368]
    r3 = lambda x: x.rearrange("q (r g j) -> q r g j", g=4, j=4)
    r2 = lambda x, g=4: x.rearrange("q (r g) -> q r g", g=g)
    sbv = r3(sb)
    VS(VV.tensor_tensor(sb, sc32, cbb, op=A.add))
    A_ = r2(tA, 1); B_ = r2(tB, 1); a_ = r2(ta, 1); b_ = r2(tb, 1)
    gamv = r2(gam)
    p1 = r2(pen)
    q1 = r2(msk[:, 0:16])
    p2 = r2(msk[:, 16:32])
    q2 = r2(msk[:, 32:48])
    VS(VV.tensor_tensor(p1, sbv[:, :, :, 0], sbv[:, :, :, 1], op=A.max))
    VS(VV.tensor_tensor(q1, sbv[:, :, :, 0], sbv[:, :, :, 1], op=A.min))
    VS(VV.tensor_tensor(p2, sbv[:, :, :, 2], sbv[:, :, :, 3], op=A.max))
    VS(VV.tensor_tensor(q2, sbv[:, :, :, 2], sbv[:, :, :, 3], op=A.min))
    VS(VV.tensor_tensor(gamv, p1, p2, op=A.add))
    VS(VV.tensor_tensor(p1, p1, q1, op=A.add))
    VS(VV.tensor_tensor(p2, p2, q2, op=A.add))
    VS(VV.tensor_tensor(p1, p1, p2, op=A.max))
    VS(VV.tensor_tensor(gamv, gamv, p1, op=A.max))
    g_ = lambda j: gamv[:, :, j:j + 1]
    VS(VV.tensor_tensor(A_, g_(0), g_(1), op=A.max))
    VS(VV.tensor_tensor(a_, g_(0), g_(1), op=A.min))
    VS(VV.tensor_tensor(B_, g_(2), g_(3), op=A.max))
    VS(VV.tensor_tensor(b_, g_(2), g_(3), op=A.min))
    thv = r2(th, 1)
    gev = r2(ge, 1); selv = r2(sel, 1); t1v = r2(tmp1, 1)
    VS(VV.tensor_tensor(thv, A_, B_, op=A.min))
    VS(VV.tensor_tensor(gev, A_, B_, op=A.is_ge))
    VS(VV.tensor_tensor(selv, gev, a_, op=A.mult))
    VS(VV.tensor_scalar(tmp1, ge, 1.0, -1.0, op0=A.subtract, op1=A.mult))
    VS(VV.tensor_tensor(t1v, t1v, b_, op=A.mult))
    VS(VV.tensor_tensor(selv, selv, t1v, op=A.add))
    VS(VV.tensor_tensor(thv, thv, selv, op=A.max))
    gmv = r2(msk[:, 48:64])
    VS(VV.tensor_tensor(gmv, gamv, thv.broadcast_to((128, 4, 4)), op=A.is_ge))
    penv = r2(pen)
    VS(VV.tensor_scalar(pen, msk[:, 48:64], 1.0, BIG, op0=A.subtract, op1=A.mult))
    curv3 = r3(cur)
    for j in range(4):
        VS(VV.tensor_tensor(curv3[:, :, :, j], sbv[:, :, :, j], penv, op=A.add))
    VS(VV.tensor_copy(sb, cur))
    curv = r2(cur, 16)
    mxv = r2(mx, 1)
    trev = r2(tre, 8)
    mskv = r2(msk, 16)

    def rmax16(dst):
        VS(VV.tensor_tensor(trev, curv[:, :, 0:8], curv[:, :, 8:16], op=A.max))
        VS(VV.tensor_tensor(trev[:, :, 0:4], trev[:, :, 0:4], trev[:, :, 4:8],
                            op=A.max))
        VS(VV.tensor_tensor(trev[:, :, 0:2], trev[:, :, 0:2], trev[:, :, 2:4],
                            op=A.max))
        VS(VV.tensor_tensor(dst, trev[:, :, 0:1], trev[:, :, 1:2], op=A.max))

    for rnd in range(3):
        rmax16(mxv)
        VS(VV.tensor_tensor(mskv, curv, mxv.broadcast_to((128, 4, 16)), op=A.is_ge))
        VS(VV.scalar_tensor_tensor(cur, msk, -BIG, cur, op0=A.mult, op1=A.add))
    rmax16(mxv)
    VS(VV.tensor_tensor(mskv, r2(sb, 16), mxv.broadcast_to((128, 4, 16)),
                        op=A.is_ge))
    VS(VV.tensor_tensor(cur, msk, sc32, op=A.mult))
    dnv = r2(dn, 1)
    VS(VV.tensor_tensor(trev, curv[:, :, 0:8], curv[:, :, 8:16], op=A.add))
    VS(VV.tensor_tensor(trev[:, :, 0:4], trev[:, :, 0:4], trev[:, :, 4:8], op=A.add))
    VS(VV.tensor_tensor(trev[:, :, 0:2], trev[:, :, 0:2], trev[:, :, 2:4], op=A.add))
    VS(VV.tensor_tensor(dnv, trev[:, :, 0:1], trev[:, :, 1:2], op=A.add))
    VS(VV.tensor_scalar(dn, dn, 1e-20, None, op0=A.add))
    VS(VV.reciprocal(rcp, dn))
    VS(VV.tensor_tensor(curv, curv, r2(rcp, 1).broadcast_to((128, 4, 16)),
                        op=A.mult))



def _host_layouts(cfg, inputs):
    NSEQ, T, D, H, HD, E, EH, HS = (cfg[k] for k in
                                    ("NSEQ", "T", "D", "H", "HD", "E", "EH", "HS"))
    DC, HP, HHD = D // 128, H // 2, H * HD
    HC = HHD // 128
    MB = HS // 128
    EC = EH // 128
    W_in = np.asarray(inputs["L0_attn_W_in"], np.float32)
    b_in = np.asarray(inputs["L0_attn_b_in"], np.float32)
    W_out = np.asarray(inputs["L0_attn_W_out"], np.float32)
    b_out = np.asarray(inputs["L0_attn_b_out"], np.float32)
    rw = np.asarray(inputs["L0_ffn_router_w"], np.float32)
    cb = np.asarray(inputs["L0_ffn_correction_bias"], np.float32)
    gw = np.asarray(inputs["L0_ffn_gate_w"], np.float32)
    uw = np.asarray(inputs["L0_ffn_up_w"], np.float32)
    dw = np.asarray(inputs["L0_ffn_down_w"], np.float32)
    sg = np.asarray(inputs["L0_ffn_shared_gate_w"], np.float32)
    su = np.asarray(inputs["L0_ffn_shared_up_w"], np.float32)
    sd = np.asarray(inputs["L0_ffn_shared_down_w"], np.float32)

    def lhsT_cols(Wt):
        K, n = Wt.shape
        return np.ascontiguousarray(
            Wt.reshape(K // 128, 128, n).transpose(1, 0, 2).reshape(128, -1))

    w4 = W_in.reshape(D, H, 4, HD)
    sec = {si: np.ascontiguousarray(w4[:, :, si, :].reshape(D, HHD))
           for si in range(4)}
    qkg_parts = []
    for hp in range(HP):
        for si in (0, 1, 3):
            qkg_parts.append(lhsT_cols(sec[si][:, hp * 128:(hp + 1) * 128]))
    wf_qkg = np.concatenate(qkg_parts, axis=1)
    wf_v = lhsT_cols(sec[2])
    wf_out = lhsT_cols(W_out)
    wf_rw = lhsT_cols(rw.T)

    b4 = b_in.reshape(H, 4, HD)
    bsec = {si: np.ascontiguousarray(b4[:, si, :].reshape(HHD)) for si in range(4)}
    col = lambda b: np.ascontiguousarray(b.reshape(-1, 128).T)
    bq_col, bk_col, bg_col = col(bsec[0]), col(bsec[1]), col(bsec[3])
    HPn = bq_col.shape[1]
    bqkg_col = np.zeros((128, HPn * 3), np.float32)
    for hp in range(HPn):
        bqkg_col[:, hp * 3 + 0] = bq_col[:, hp]
        bqkg_col[:, hp * 3 + 1] = bk_col[:, hp]
        bqkg_col[:, hp * 3 + 2] = bg_col[:, hp]
    bv_bc = np.ascontiguousarray(np.broadcast_to(bsec[2], (128, HHD)))
    bo_col = np.ascontiguousarray(b_out.reshape(DC, 128).T)
    cb_bc = np.ascontiguousarray(np.broadcast_to(np.tile(cb, 4), (128, 64)))

    parts = []
    for m in range(MB):
        parts.append(lhsT_cols(np.ascontiguousarray(sg.T[:, m * 128:(m + 1) * 128])))
        parts.append(lhsT_cols(np.ascontiguousarray(su.T[:, m * 128:(m + 1) * 128])))
    for e in range(E):
        parts.append(lhsT_cols(gw[e]))
        parts.append(lhsT_cols(uw[e]))
    sdT = np.ascontiguousarray(sd.T)
    for ct in range(DC):
        parts.append(lhsT_cols(np.ascontiguousarray(sdT[:, ct * 128:(ct + 1) * 128])))
        for e in range(E):
            parts.append(lhsT_cols(np.ascontiguousarray(
                dw[e][:, ct * 128:(ct + 1) * 128])))
    wf_ffn = np.concatenate(parts, axis=1)

    return dict(wf_qkg=wf_qkg, wf_v=wf_v, wf_out=wf_out, wf_rw=wf_rw, wf_ffn=wf_ffn,
                bqkg_col=bqkg_col, bv_bc=bv_bc,
                bo_col=bo_col, cb_bc=cb_bc,
                idm32=np.eye(128, dtype=np.float32))


def kernel(**inputs):
    cfg = CFG_FULL
    NSEQ, T, D = cfg["NSEQ"], cfg["T"], cfg["D"]
    X = np.asarray(inputs["L0_attn_X"], np.float32)
    B = X.shape[0]
    ncores = 8
    assert B == ncores * NSEQ
    host = _host_layouts(cfg, inputs)
    nc = build_program(cfg)
    in_maps = []
    for c in range(ncores):
        xs = X[c * NSEQ:(c + 1) * NSEQ].transpose(0, 2, 1)
        m = dict(host)
        m["x_in"] = np.ascontiguousarray(xs.reshape(NSEQ * D, T))
        in_maps.append(m)
    res = run_bass_kernel_spmd(nc, in_maps, core_ids=list(range(ncores)))
    outs = []
    for c in range(ncores):
        y = res.results[c]["y_out"].reshape(NSEQ, D, T).transpose(0, 2, 1)
        outs.append(y)
    return np.ascontiguousarray(np.concatenate(outs, axis=0))


# version 17
# speedup vs baseline: 1.0201x; 1.0199x over previous
import numpy as np
import ml_dtypes
from contextlib import ExitStack
import bass_rust
import concourse.bass as bass
import concourse.mybir as mybir
from concourse.bass_utils import run_bass_kernel_spmd
from concourse.alu_op_type import AluOpType as ALU

F32 = mybir.dt.float32
BF16 = mybir.dt.bfloat16
FP16 = mybir.dt.float16
AF = mybir.ActivationFunctionType

EPS_A = 1e-4
LN_EPS = 1e-5
BIG = 1e9
PRIME = 1 << 16

CFG_FULL = dict(NSEQ=4, T=2048, D=1024, H=16, HD=64, E=16, EH=256, HS=2048)

CONSTS = [0.0, 1.0, -1.0, 2.0, -2.0, 0.25, 0.5, EPS_A, 2 * EPS_A, EPS_A * EPS_A,
          LN_EPS, 1e-20, BIG, -BIG]


def dsl(base, i, w):
    if isinstance(i, int):
        s0 = base + i * w
        return slice(s0, s0 + w)
    return bass_rust.DynSlice(i * w + base, w)


def dsl2(base, i, step, w):
    if isinstance(i, int):
        s0 = base + i * step
        return slice(s0, s0 + w)
    return bass_rust.DynSlice(i * step + base, w)


class P:

    ENG = ("T", "A", "V", "G", "S")

    def __init__(self, nc, ctx):
        self.nc = nc
        self.ctx = ctx
        self.eng = {"T": nc.tensor, "A": nc.scalar, "V": nc.vector,
                    "G": nc.gpsimd, "S": nc.sync}
        self.sems = {}
        self.cnt = {}
        self.regs = {}
        self.tmp = {e: self.eng[e].alloc_register(f"tmp_{e}") for e in self.ENG}
        self.loops = []

    def sem(self, name, waiters=("T", "A", "V", "G", "S"), prime=True):
        h = self.ctx.enter_context(self.nc.semaphore(name))
        self.sems[name] = h
        for e in waiters:
            r = self.eng[e].alloc_register(f"r_{name}_{e}")
            self.eng[e].reg_mov(r, 0)
            self.regs[(name, e)] = r
        if prime:
            self.eng["S"].sem_inc(h, PRIME)
            self.cnt[name] = PRIME
        else:
            self.cnt[name] = 0
        return h

    def inc(self, instr, name, n=1):
        instr.then_inc(self.sems[name], n)
        self.cnt[name] += n
        return instr

    def wait(self, e, name, lag=0):
        tgt = self.cnt[name] + lag
        eng = self.eng[e]
        r = self.regs[(name, e)]
        eng.reg_add(self.tmp[e], r, tgt)
        eng.wait_ge(self.sems[name], self.tmp[e])

    def loop(self, n):
        prog = self

        class _L:
            def __enter__(sl):
                sl.fori = prog.nc.Fori(0, n)
                i = sl.fori.__enter__()
                prog.loops.append({s: prog.cnt[s] for s in prog.sems})
                return i

            def __exit__(sl, *a):
                entry = prog.loops.pop()
                deltas = {s: prog.cnt[s] - entry[s] for s in prog.sems}
                for (s, e), r in prog.regs.items():
                    if deltas[s]:
                        prog.eng[e].reg_add(r, r, deltas[s])
                sl.fori.__exit__(*a)
                for (s, e), r in prog.regs.items():
                    if deltas[s]:
                        prog.eng[e].reg_add(r, r, -deltas[s] * n)
                for s, d in deltas.items():
                    if d:
                        prog.cnt[s] += d * (n - 1)
        return _L()


def build_program(cfg, use_repl_dma=False, dump_internal=False):
    NSEQ, T, D, H, HD, E, EH, HS = (cfg[k] for k in
                                    ("NSEQ", "T", "D", "H", "HD", "E", "EH", "HS"))
    DC = D // 128
    HP = H // 2
    SC = T // 128
    T4 = T // 512
    TJ = T // 1024
    HHD = H * HD
    HC = HHD // 128
    MB = HS // 128
    EC = EH // 128
    assert E == 16 and H % 2 == 0 and T % 1024 == 0 and D % 128 == 0
    assert HD == 64 and EH % 128 == 0 and HS % 128 == 0 and HHD % 128 == 0

    QKG_COLS = HP * 3 * D
    V_COLS = DC * HHD
    OUT_COLS = HC * D
    RW_COLS = DC * 16
    SEC1 = MB * 2 * D
    EPIECE = 2 * DC * EH
    SEC2 = E * EPIECE
    SEC3C = HS + E * EC * 128
    SEC3 = DC * SEC3C
    FFN_COLS = SEC1 + SEC2 + SEC3

    nc = bass.Bass("TRN2", target_bir_lowering=False, debug=False,
                   detect_race_conditions=False)
    ap = lambda t: t.ap()
    x_in = ap(nc.dram_tensor("x_in", [NSEQ * D, T], F32, kind="ExternalInput"))
    wf_qkg = ap(nc.dram_tensor("wf_qkg", [128, QKG_COLS], F32, kind="ExternalInput"))
    wf_v = ap(nc.dram_tensor("wf_v", [128, V_COLS], F32, kind="ExternalInput"))
    wf_out = ap(nc.dram_tensor("wf_out", [128, OUT_COLS], F32, kind="ExternalInput"))
    wf_rw = ap(nc.dram_tensor("wf_rw", [128, RW_COLS], F32, kind="ExternalInput"))
    wf_ffn = ap(nc.dram_tensor("wf_ffn", [128, FFN_COLS], F32, kind="ExternalInput"))
    bqkg_col = ap(nc.dram_tensor("bqkg_col", [128, HP * 3], F32,
                                 kind="ExternalInput"))
    bv_bc = ap(nc.dram_tensor("bv_bc", [128, HHD], F32, kind="ExternalInput"))
    bo_col = ap(nc.dram_tensor("bo_col", [128, DC], F32, kind="ExternalInput"))
    cb_bc = ap(nc.dram_tensor("cb_bc", [128, 64], F32, kind="ExternalInput"))
    idm32 = ap(nc.dram_tensor("idm32", [128, 128], F32, kind="ExternalInput"))
    y_out = ap(nc.dram_tensor("y_out", [NSEQ * D, T], F32, kind="ExternalOutput"))
    wb_qkg = ap(nc.dram_tensor("wb_qkg", [128, QKG_COLS], BF16))
    wb_v = ap(nc.dram_tensor("wb_v", [128, V_COLS], BF16))
    wb_out = ap(nc.dram_tensor("wb_out", [128, OUT_COLS], BF16))
    wb_rw = ap(nc.dram_tensor("wb_rw", [128, RW_COLS], BF16))
    wb_ffn = ap(nc.dram_tensor("wb_ffn", [128, FFN_COLS], BF16))
    _ik = dict(kind="ExternalOutput") if dump_internal else {}
    h_hbm = ap(nc.dram_tensor("h_hbm", [HHD, T], BF16, **_ik))
    x1_hbm = ap(nc.dram_tensor("x1_hbm", [D, T], F32, **_ik))
    if dump_internal:
        dbg16 = ap(nc.dram_tensor("dbg16", [128, 40960], BF16, kind="ExternalOutput"))
        dbg32 = ap(nc.dram_tensor("dbg32", [128, 1024], F32, kind="ExternalOutput"))

    with ExitStack() as ctx:
        En = ctx.enter_context
        p = P(nc, ctx)
        TT, AA, VV, GG, SS = nc.tensor, nc.scalar, nc.vector, nc.gpsimd, nc.sync

        cp32 = En(nc.sbuf_tensor("cp32", [128, len(CONSTS)], F32))
        cp16 = En(nc.sbuf_tensor("cp16", [128, len(CONSTS)], BF16))
        ones_r16 = En(nc.sbuf_tensor("ones_r16", [128, 128], BF16))
        ones_r32 = En(nc.sbuf_tensor("ones_r32", [128, 128], F32))
        ones_c16 = En(nc.sbuf_tensor("ones_c16", [128, 2], BF16))
        onesw = En(nc.sbuf_tensor("onesw", [128, 2048], BF16))
        idt32 = En(nc.sbuf_tensor("idt32", [128, 128], F32))
        bqkg = En(nc.sbuf_tensor("bqkg", [128, HP * 3], F32))
        bstg = En(nc.sbuf_tensor("bstg", [128, 4], F32))
        bvb = En(nc.sbuf_tensor("bvb", [128, HHD], F32))
        boc = En(nc.sbuf_tensor("boc", [128, DC], F32))
        cbb = En(nc.sbuf_tensor("cbb", [128, 64], F32))
        rww = En(nc.sbuf_tensor("rww", [128, RW_COLS], BF16))
        xbn = En(nc.sbuf_tensor("xbn", [128, DC * T], BF16))

        XF_O = 0
        ST_O = XF_O + DC * 512
        X1_O = ST_O + 2048
        OB_O = X1_O
        RT_O = OB_O + DC * 512
        LG_O = OB_O
        AF32_COLS = max(RT_O + 384, 2 * 4096)
        ar32 = En(nc.sbuf_tensor("ar32", [128, AF32_COLS], F32))

        o = 0
        VAUG_O = o; o += HP * SC * 130
        assert VAUG_O == 0
        QR_O = [o, o + T]; o += 2 * T
        KL_O = [o, o + T]; o += 2 * T
        GH_O = [o, o + T]; o += 2 * T
        PSCR_O = o; o += 3 * T
        SQ_O = o; o += T
        ROW_O = o; o += max(2 * T, DC * 512 - T + 1024)
        RB_O = [o, o + 1024]; o += 2048
        WT_O = [o, o + 1024]; o += 2048
        ZB_OFF = None
        HST_O = o; o += T
        RHO_O = o; o += 1024
        RHOB_O = o; o += 512
        AEV_O = o; o += 512
        WPAIR_O = [o, o]; o += 3 * D
        WBIG_O = o; o += max(V_COLS, OUT_COLS)
        WT_O.append(WBIG_O)
        VSTG_O = [o, o]; o += SC * 130
        ZB_OFF = [SQ_O, ROW_O + 2048, RHO_O]
        att_cols = o
        o = 0
        SH_O = o; o += MB * 512
        assert SH_O == 0
        HW_O = o; o += E * EC * 512
        assert HW_O % 512 == 0
        MSG_O = [o, o + 2 * D]; o += 4 * D
        EB_O = [o, o + EPIECE]; o += 2 * EPIECE
        CBF_O = [o, o + SEC3C]; o += 2 * SEC3C
        BCALL_O = o; o += E * 512
        CMB16_O = o; o += 64
        CMBF_O = o; o += 512
        FT1_O = o; o += EC * 512
        XSTG_O = o; o += DC * 512
        ffn_cols = o
        AB16_COLS = max(att_cols, ffn_cols, 2 * 8192)
        ar16 = En(nc.sbuf_tensor("ar16", [128, AB16_COLS], BF16))

        def a32(start, n):
            return ar32[:, start:start + n]

        def a16(start, n):
            return ar16[:, start:start + n]

        def warm(psbank):
            TT.matmul(psbank[0:1, 0:32], ones_c16[0:1, 0:1], onesw[0:1, 0:32],
                      start=True, stop=True)

        pd2a = En(nc.psum_tensor("pd2a", [128, 1024], F32))
        pd2b = En(nc.psum_tensor("pd2b", [128, 1024], F32))
        pA0 = En(nc.psum_tensor("pA0", [128, 512], F32))
        pA1 = En(nc.psum_tensor("pA1", [128, 512], F32))
        pX = En(nc.psum_tensor("pX", [128, 512], F32))
        pY = En(nc.psum_tensor("pY", [128, 512], F32))

        p.sem("sT", waiters=("A", "V", "G", "S"))
        p.sem("sA", waiters=("T", "A", "V", "G", "S"))
        p.sem("sV", waiters=("T", "A", "V", "G", "S"))
        p.sem("dInit", waiters=("T", "V"), prime=False)
        p.sem("dCf0", waiters=("V",), prime=False); p.sem("dCf1", waiters=("V",), prime=False)
        p.sem("dCb0", waiters=("V", "S"), prime=False); p.sem("dCb1", waiters=("V", "S"), prime=False)
        p.sem("dXf", waiters=("V",), prime=False)
        p.sem("dHb", waiters=("T",), prime=False)
        p.sem("dWbig", waiters=("T",), prime=False)
        p.sem("dWp0", waiters=("T",), prime=False); p.sem("dWp1", waiters=("T",), prime=False)
        p.sem("dMsg0", waiters=("T",), prime=False); p.sem("dMsg1", waiters=("T",), prime=False)
        p.sem("dEb0", waiters=("T",), prime=False); p.sem("dEb1", waiters=("T",), prime=False)
        p.sem("dCb20", waiters=("T",), prime=False); p.sem("dCb21", waiters=("T",), prime=False)
        p.sem("dShift", waiters=("T", "V"), prime=False)
        p.sem("dRho", waiters=("T", "V"), prime=False)
        p.sem("dRow", waiters=("T",), prime=False)
        p.sem("dH", waiters=("S",), prime=False)
        p.sem("dX1", waiters=("S", "V"), prime=False)
        p.sem("dY", waiters=("G", "V", "A"), prime=False)
        p.sem("dVstg", waiters=("T",), prime=False)

        def dmaS(sem, out, in_, **kw):
            p.inc(SS.dma_start(out=out, in_=in_, **kw), sem, 16)

        def dmaG(sem, out, in_, **kw):
            p.inc(GG.dma_start(out=out, in_=in_, **kw), sem, 16)

        C32 = {v: cp32[:, i:i + 1] for i, v in enumerate(CONSTS)}
        for i, v in enumerate(CONSTS):
            nc.const_aps.register(F32, v, C32[v])
            nc.const_aps.register(BF16, v, cp16[:, i:i + 1])

        for i, v in enumerate(CONSTS):
            VV.memset(cp32[:, i:i + 1], v)
            VV.memset(cp16[:, i:i + 1], v)
        VV.memset(ones_r16[0:1, :], 1.0)
        VV.memset(ones_r16[64:65, :], 1.0)
        VV.memset(ones_r32[0:1, :], 1.0)
        VV.memset(ones_r32[64:65, :], 1.0)
        VV.memset(ones_c16[:, :], 1.0)
        VV.memset(ar16[:, att_cols:att_cols + 64], 0.0)
        p.inc(VV.memset(onesw[:, :], 1.0), "sV")
        dmaS("dInit", idt32[:], idm32)
        dmaS("dInit", bqkg[:], bqkg_col); dmaS("dInit", bvb[:], bv_bc)
        dmaS("dInit", boc[:], bo_col); dmaS("dInit", cbb[:], cb_bc)

        CH = 4096
        cast_jobs = []
        for src, dst, cols in ((wf_qkg, wb_qkg, QKG_COLS), (wf_v, wb_v, V_COLS),
                               (wf_out, wb_out, OUT_COLS), (wf_rw, wb_rw, RW_COLS),
                               (wf_ffn, wb_ffn, FFN_COLS)):
            c0 = 0
            while c0 < cols:
                w = min(CH, cols - c0)
                cast_jobs.append((src, dst, c0, w))
                c0 += w
        for j, (src, dst, c0, w) in enumerate(cast_jobs):
            par = j % 2
            fbuf = a32(par * 4096, w)
            bbuf = a16(par * 4096, w)
            if par == 0:
                if j >= 2:
                    p.wait("S", "sV", -1)
                dmaS(f"dCf{par}", fbuf, src[:, c0:c0 + w])
            else:
                if j >= 2:
                    p.wait("A", "sV", -1)
                p.inc(AA.dma_start(out=fbuf, in_=src[:, c0:c0 + w]),
                      f"dCf{par}", 16)
            p.wait("V", f"dCf{par}")
            if j >= 2:
                p.wait("V", f"dCb{par}")
            p.inc(VV.tensor_copy(bbuf, fbuf), "sV")
            p.wait("G", "sV")
            dmaG(f"dCb{par}", dst[:, c0:c0 + w], bbuf)
        p.wait("S", "dCb0")
        p.wait("S", "dCb1")
        dmaS("dInit", rww[:], wb_rw)
        p.wait("T", "dInit")
        p.wait("V", "dInit")
        nc.all_engine_barrier()

        def xsl(k, tt, width=512):
            return xbn[:, dsl(k * T, tt, width)] if width == 512 else \
                xbn[:, k * T + tt * 1024: k * T + (tt + 1) * 1024]

        def ln_phase(src_tile):
            XO = (XF_O, X1_O)
            LSEM = ("dXf", "dRho")

            def ld(tt):
                p.wait("S", "sV")
                if tt <= 1:
                    p.wait("S", "dX1")
                dmaS(LSEM[tt % 2],
                     a32(XO[tt % 2], DC * 512).rearrange("q (k w) -> q k w",
                                                         k=DC), src_tile(tt))
            ld(0)
            for tt in range(T4):
                xf = a32(XO[tt % 2], DC * 512)
                p.wait("V", LSEM[tt % 2])
                for k in range(DC):
                    ins = VV.tensor_copy(xsl(k, tt), xf[:, k * 512:(k + 1) * 512])
                p.inc(ins, "sV")
                if tt + 1 < T4:
                    ld(tt + 1)
                sqb = a16(SQ_O, DC * 512)
                for k in range(DC):
                    ins = VV.tensor_tensor(sqb[:, k * 512:(k + 1) * 512],
                                           xsl(k, tt), xsl(k, tt), op=ALU.mult)
                p.inc(ins, "sV")
                p.wait("T", "sV")
                for k in range(DC):
                    TT.matmul(pX[0:1, :], ones_c16[:, 0:1], xsl(k, tt),
                              start=(k == 0), stop=(k == DC - 1))
                for k in range(DC):
                    ins = TT.matmul(pY[0:1, :], ones_c16[:, 0:1],
                                    sqb[:, k * 512:(k + 1) * 512],
                                    start=(k == 0), stop=(k == DC - 1))
                p.inc(ins, "sT")
                st = a32(ST_O, 2048)
                p.wait("V", "sT")
                VV.tensor_scalar(st[0:1, 0:512], pX[0:1, :], 1.0 / D, None,
                                 op0=ALU.mult)
                VV.tensor_tensor(st[0:1, 512:1024], st[0:1, 0:512], st[0:1, 0:512],
                                 op=ALU.mult)
                p.inc(VV.scalar_tensor_tensor(st[0:1, 1024:1536], pY[0:1, :],
                                              1.0 / D, st[0:1, 512:1024],
                                              op0=ALU.mult, op1=ALU.subtract), "sV")
                p.wait("A", "sV")
                p.inc(AA.activation(st[0:1, 1536:2048], st[0:1, 1024:1536], AF.Ln,
                                    bias=LN_EPS, scale=1.0), "sA")
                p.wait("A", "sA", -1)
                p.inc(AA.activation(st[0:1, 1024:1536], st[0:1, 1536:2048], AF.Exp,
                                    scale=-0.5), "sA")
                p.wait("V", "sA")
                mrow = a16(SQ_O + DC * 512, 512)
                p.inc(VV.tensor_copy(mrow[0:1, :], st[0:1, 0:512]), "sV")
                p.wait("T", "sV")
                TT.matmul(pX[:, :], ones_r16[0:1, :], mrow[0:1, :],
                          start=True, stop=True)
                p.inc(TT.matmul(pY[:, :], ones_r32[0:1, :], st[0:1, 1024:1536],
                                start=True, stop=True), "sT")
                p.wait("A", "sT")
                mbb = a16(SQ_O + DC * 512 + 512, 512)
                p.inc(AA.activation(mbb[:, :], pX[:, :], AF.Copy), "sA")
                p.wait("V", "sA")
                for k in range(DC):
                    VV.tensor_tensor(xsl(k, tt), xsl(k, tt), mbb[:, :],
                                     op=ALU.subtract)
                    ins = VV.tensor_tensor(xsl(k, tt), xsl(k, tt), pY[:, :],
                                           op=ALU.mult)
                p.inc(ins, "sV")
            nc.all_engine_barrier()

        def emit_A(sc, tj, vbase, u, vlag=0):
            wt = a16(WT_O[sc % 3], 1024)
            p.wait("T", "sV", vlag)
            lh = ar16[:, vbase + sc * 130 + u * 65:
                      vbase + sc * 130 + u * 65 + 128]
            TT.matmul(pA0[0:128, :], lh, wt[:, 0:512],
                      start=(sc == 0), stop=(sc == SC - 1))
            p.inc(TT.matmul(pA1[0:128, :], lh, wt[:, 512:1024],
                            start=(sc == 0), stop=(sc == SC - 1)), "sT")

        with p.loop(NSEQ) as s:
            nc.all_engine_barrier()
            x4 = x_in.rearrange("(sq k q) t -> sq q k t", q=128, k=DC)
            ln_phase(lambda tt: x4[dsl(0, s, 1), :, :, tt * 512:(tt + 1) * 512])

            for bo in (QR_O[0], QR_O[1], KL_O[0], KL_O[1]):
                ins = VV.memset(ar16[64:128, bo:bo + T], 0.0)
            p.inc(ins, "sV")

            wv = a16(WBIG_O, V_COLS)
            p.wait("S", "sT")
            dmaS("dWbig", wv, wb_v)
            vaug_full = a16(VAUG_O, HP * SC * 130)
            vview = vaug_full.rearrange("q (hp sc u d) -> q hp sc u d",
                                        hp=HP, sc=SC, u=2)
            p.inc(VV.memset(vview[:, :, :, :, 64:65], 1.0), "sV")
            p.wait("T", "dWbig")
            p.wait("T", "sV")
            NHALF = HHD // 512 if HHD >= 512 else 1
            HW_HALF = min(HHD, 512)
            grp_i = 0
            for st_ in range(SC):
                for half in range(NHALF):
                    ps = pX if grp_i % 2 == 0 else pY
                    if grp_i >= 2:
                        p.wait("T", "sV", -1)
                    for k in range(DC):
                        ins = TT.matmul(ps[:, 0:HW_HALF],
                                        xbn[:, k * T + st_ * 128:k * T + st_ * 128 + 128],
                                        wv[:, k * HHD + half * 512:
                                           k * HHD + half * 512 + HW_HALF],
                                        start=(k == 0), stop=(k == DC - 1))
                    p.inc(ins, "sT")
                    p.wait("V", "sT")
                    nhp = HW_HALF // 128
                    hp0 = half * 4
                    outv = vview[:, hp0:hp0 + nhp, st_, :, 0:64]
                    inv = ps[:, 0:HW_HALF].rearrange("q (hp u d) -> q hp u d",
                                                     hp=nhp, u=2)
                    bvv = bvb[:, half * 512:half * 512 + HW_HALF].rearrange(
                        "q (hp u d) -> q hp u d", hp=nhp, u=2)
                    p.inc(VV.tensor_tensor(outv, inv, bvv, op=ALU.add), "sV")
                    grp_i += 1
            nc.all_engine_barrier()

            with p.loop(HP // 2) as hp2:
                for u2 in range(2):
                    hp = hp2 * 2 + u2
                    wp = a16(WPAIR_O[u2], 3 * D)
                    p.wait("S", "sT", 0)
                    dmaS(f"dWp{u2}", wp, wb_qkg[:, dsl(0, hp, 3 * D)])
                    vstg = a16(VSTG_O[u2], SC * 130)
                    p.wait("V", "sT")
                    p.inc(VV.tensor_copy(bstg[:, 0:3], bqkg[:, dsl(0, hp, 3)]), "sV")
                    p.inc(VV.tensor_copy(vstg, ar16[:, dsl(0, hp, SC * 130)]), "sV")
                    p.wait("V", "sV")
                    p.wait("T", f"dWp{u2}")
                    gi = 0
                    for si, (dst0, scl) in enumerate(
                            ((QR_O, -2.0), (KL_O, 1.0), (GH_O, 1.0))):
                        for tt in range(T4):
                            ps = pX if gi % 2 == 0 else pY
                            p.wait("T", "sV", -1 if gi >= 2 else -2)
                            for k in range(DC):
                                ins = TT.matmul(
                                    ps[:, :],
                                    wp[:, si * D + k * 128:si * D + (k + 1) * 128],
                                    xsl(k, tt), start=(k == 0), stop=(k == DC - 1))
                            p.inc(ins, "sT")
                            p.wait("V", "sT")
                            bcl = bstg[:, si:si + 1]
                            d0 = ar16[0:64, dst0[0] + tt * 512:dst0[0] + (tt + 1) * 512]
                            d1 = ar16[64:128, PSCR_O + si * T + tt * 512:
                                      PSCR_O + si * T + (tt + 1) * 512]
                            if scl == 1.0:
                                VV.tensor_scalar(d0, ps[0:64, :], bcl[0:64, :], None,
                                                 op0=ALU.add)
                                p.inc(VV.tensor_scalar(d1, ps[64:128, :],
                                                       bcl[64:128, :], None,
                                                       op0=ALU.add), "sV")
                            else:
                                VV.tensor_scalar(d0, ps[0:64, :], bcl[0:64, :], scl,
                                                 op0=ALU.add, op1=ALU.mult)
                                p.inc(VV.tensor_scalar(d1, ps[64:128, :],
                                                       bcl[64:128, :], scl,
                                                       op0=ALU.add, op1=ALU.mult),
                                      "sV")
                            gi += 1
                    p.wait("G", "sV")
                    dmaG("dShift", ar16[0:64, QR_O[1]:QR_O[1] + T],
                         ar16[64:128, PSCR_O:PSCR_O + T])
                    dmaG("dShift", ar16[0:64, KL_O[1]:KL_O[1] + T],
                         ar16[64:128, PSCR_O + T:PSCR_O + 2 * T])
                    dmaG("dShift", ar16[0:64, GH_O[1]:GH_O[1] + T],
                         ar16[64:128, PSCR_O + 2 * T:PSCR_O + 3 * T])
                    for uu in range(2):
                        dmaG("dShift", ar16[66:68, QR_O[uu]:QR_O[uu] + T],
                             onesw[0:2, 0:T])
                        dmaG("dShift", ar16[64:66, KL_O[uu]:KL_O[uu] + T],
                             onesw[0:2, 0:T])

                    for u in range(2):
                        qr = ar16[:, QR_O[u]:QR_O[u] + T]
                        kl = ar16[:, KL_O[u]:KL_O[u] + T]
                        gh = ar16[:, GH_O[u]:GH_O[u] + T]
                        if u == 1:
                            p.wait("V", "dShift")
                        sqs = a16(SQ_O, T)
                        rows = a16(ROW_O, 2 * T)
                        for (tile, qscale, hoff) in ((qr, 0.25, 0), (kl, 1.0, 1)):
                            p.inc(VV.tensor_tensor(sqs[0:64, :], tile[0:64, :],
                                                   tile[0:64, :], op=ALU.mult), "sV")
                            p.wait("T", "sV")
                            for tt in range(T4):
                                pdx = pd2a if tt < 2 else pd2b
                                col = (tt % 2) * 512
                                ins = TT.matmul(pdx[0:1, col:col + 512],
                                                ones_c16[0:64, 0:1],
                                                sqs[0:64, tt * 512:(tt + 1) * 512],
                                                start=True, stop=True)
                            p.inc(ins, "sT")
                            p.wait("A", "sT")
                            W1 = min(T4, 2) * 512
                            ins = AA.activation(rows[0:1, 0:W1], pd2a[0:1, 0:W1],
                                                AF.Copy, scale=qscale)
                            if T4 > 2:
                                ins = AA.activation(rows[0:1, 1024:2048],
                                                    pd2b[0:1, :], AF.Copy,
                                                    scale=qscale)
                            p.inc(ins, "sA")
                            p.wait("V", "sA")
                            ins = VV.scalar_tensor_tensor(rows[0:1, T:T + W1],
                                                          pd2a[0:1, 0:W1], qscale,
                                                          rows[0:1, 0:W1],
                                                          op0=ALU.mult,
                                                          op1=ALU.subtract)
                            if T4 > 2:
                                ins = VV.scalar_tensor_tensor(
                                    rows[0:1, T + 1024:T + 2048], pd2b[0:1, :],
                                    qscale, rows[0:1, 1024:2048],
                                    op0=ALU.mult, op1=ALU.subtract)
                            p.inc(ins, "sV")
                            p.wait("G", "sV")
                            if hoff == 0:
                                dmaG("dShift", qr[64:65, :], rows[0:1, 0:T])
                                dmaG("dShift", qr[65:66, :], rows[0:1, T:2 * T])
                            else:
                                dmaG("dShift", kl[66:67, :], rows[0:1, 0:T])
                                dmaG("dShift", kl[67:68, :], rows[0:1, T:2 * T])
                        p.wait("T", "dShift")

                        vau = a16(VSTG_O[u2], SC * 130)
                        if u == 0:
                            p.wait("T", "sV")
                        for tj in range(TJ):
                            def stage_T(sc):
                                par = sc % 2
                                pd = pd2a if par == 0 else pd2b
                                if sc == 0:
                                    p.wait("V", "dShift")
                                p.wait("T", "sA", 0 if sc <= 1 else -2)
                                if sc <= 1:
                                    p.wait("T", "sV")
                                TT.matmul(pd[:, 0:512],
                                          kl[0:128, sc * 128:(sc + 1) * 128],
                                          qr[0:128, tj * 1024:tj * 1024 + 512],
                                          start=True, stop=True)
                                p.inc(TT.matmul(
                                    pd[:, 512:1024], kl[0:128, sc * 128:(sc + 1) * 128],
                                    qr[0:128, tj * 1024 + 512:tj * 1024 + 1024],
                                    start=True, stop=True), "sT")

                            def stage_ACT(sc, skew):
                                if sc < SC:
                                    par = sc % 2
                                    pd = pd2a if par == 0 else pd2b
                                    lb = ar16[:, RB_O[par]:RB_O[par] + 1024]\
                                        .bitcast(FP16)[:, 0:1024]
                                    p.wait("A", "sT", -1 if skew else 0)
                                    p.inc(AA.activation(lb, pd[:, :], AF.Ln,
                                                        bias=EPS_A * EPS_A,
                                                        scale=1.0), "sA")
                                if sc >= 1:
                                    parp = (sc - 1) % 2
                                    lbp = ar16[:, RB_O[parp]:RB_O[parp] + 1024]\
                                        .bitcast(FP16)[:, 0:1024]
                                    zb = a16(ZB_OFF[(sc - 1) % 3], 1024)
                                    p.inc(AA.activation(zb, lbp, AF.Exp,
                                                        scale=-0.5), "sA")

                            def stage_V(sc, tlag=-2):
                                par = sc % 2
                                zb = a16(ZB_OFF[sc % 3], 1024)
                                bp = a16(ROW_O + par * 1024, 1024)
                                wt = a16(WT_O[sc % 3], 1024)
                                p.wait("V", "sA")
                                p.wait("V", "sT", tlag)
                                with nc.allow_low_precision(reason="bf16 attn wts"):
                                    VV.tensor_scalar(bp, zb, -EPS_A, 1.0,
                                                     op0=ALU.mult, op1=ALU.add)
                                    VV.tensor_tensor(bp, zb, bp, op=ALU.mult)
                                    p.inc(VV.tensor_tensor(wt, bp, bp, op=ALU.mult),
                                          "sV")

                            for sc in range(SC):
                                stage_T(sc)
                                if sc >= 3:
                                    emit_A(sc - 3, tj, VSTG_O[u2], u, vlag=-1)
                                stage_ACT(sc, skew=(sc >= 3))
                                if sc >= 1:
                                    stage_V(sc - 1)
                            p.wait("A", "sV", -1)
                            stage_ACT(SC, skew=True)
                            stage_V(SC - 1, tlag=0)
                            emit_A(SC - 3, tj, VSTG_O[u2], u, vlag=-2)
                            emit_A(SC - 2, tj, VSTG_O[u2], u, vlag=-1)
                            emit_A(SC - 1, tj, VSTG_O[u2], u)
                            rho0 = a16(RHO_O, 1024)
                            rho1 = a16(RHOB_O, 1024)
                            p.wait("A", "sT")
                            p.inc(AA.activation(rho0[64:65, 0:512], pA0[64:65, :],
                                                AF.Ln, bias=EPS_A, scale=1.0), "sA")
                            p.inc(AA.activation(rho1[64:65, 0:512], pA1[64:65, :],
                                                AF.Ln, bias=EPS_A, scale=1.0), "sA")
                            p.wait("A", "sA", -1)
                            p.inc(AA.activation(rho0[64:65, 512:1024],
                                                rho0[64:65, 0:512],
                                                AF.Exp, scale=-1.0), "sA")
                            p.wait("A", "sA", -1)
                            p.inc(AA.activation(rho1[64:65, 512:1024],
                                                rho1[64:65, 0:512],
                                                AF.Exp, scale=-1.0), "sA")
                            p.wait("T", "sV")
                            p.wait("T", "sA", -1)
                            p.inc(TT.matmul(pX[0:64, :], ones_r16[64:65, 0:64],
                                            rho0[64:65, 512:1024],
                                            start=True, stop=True), "sT")
                            p.wait("T", "sA")
                            p.inc(TT.matmul(pY[0:64, :], ones_r16[64:65, 0:64],
                                            rho1[64:65, 512:1024],
                                            start=True, stop=True), "sT")
                            hs = lambda v: ar16[0:64,
                                                HST_O + (tj * 2 + v) * 512:
                                                HST_O + (tj * 2 + v + 1) * 512]
                            ghs = lambda v: gh[0:64, tj * 1024 + v * 512:
                                               tj * 1024 + (v + 1) * 512]
                            p.wait("V", "sT", -2)
                            VV.tensor_tensor(hs(0), pA0[0:64, :], ghs(0),
                                             op=ALU.mult)
                            p.wait("V", "sT", -1)
                            VV.tensor_tensor(hs(0), hs(0), pX[0:64, :],
                                             op=ALU.mult)
                            VV.tensor_tensor(hs(1), pA1[0:64, :], ghs(1),
                                             op=ALU.mult)
                            p.wait("V", "sT")
                            p.inc(VV.tensor_tensor(hs(1), hs(1), pY[0:64, :],
                                                   op=ALU.mult), "sV")
                        p.wait("G", "sV")
                        dmaG("dH", h_hbm[dsl(0, hp * 2 + u, HD), :],
                             ar16[0:64, HST_O:HST_O + T])
            nc.all_engine_barrier()

            wo = a16(WBIG_O, OUT_COLS)
            p.wait("S", "dH")
            p.wait("S", "sT")
            dmaS("dWbig", wo, wb_out)
            hv = h_hbm.rearrange("(k q) t -> q k t", q=128)
            x1v = x1_hbm.rearrange("(k q) t -> q k t", q=128)
            for tt in range(T4):
                xf = a32(XF_O, DC * 512)
                p.wait("S", "sV")
                dmaS("dXf", xf.rearrange("q (k w) -> q k w", k=DC),
                     x4[dsl(0, s, 1), :, :, tt * 512:(tt + 1) * 512])
                hbuf = a16(QR_O[0], HC * 512)
                p.wait("S", "sT")
                dmaS("dHb", hbuf.rearrange("q (k w) -> q k w", k=HC),
                     hv[:, :, tt * 512:(tt + 1) * 512])
                p.wait("T", "dWbig")
                p.wait("T", "dHb")
                x1sl = a32(X1_O, DC * 512)
                p.wait("V", "dX1")
                for ct in range(DC):
                    ps = pX if ct % 2 == 0 else pY
                    if ct >= 2:
                        p.wait("T", "sV", -1)
                    for k in range(HC):
                        ins = TT.matmul(ps[:, :],
                                        wo[:, k * D + ct * 128:k * D + (ct + 1) * 128],
                                        hbuf[:, k * 512:(k + 1) * 512],
                                        start=(k == 0), stop=(k == HC - 1))
                    p.inc(ins, "sT")
                    p.wait("V", "sT")
                    if ct == 0:
                        p.wait("V", "dXf")
                    p.inc(VV.scalar_tensor_tensor(
                        x1sl[:, ct * 512:(ct + 1) * 512], ps[:, :], boc[:, ct:ct + 1],
                        xf[:, ct * 512:(ct + 1) * 512], op0=ALU.add, op1=ALU.add),
                        "sV")
                p.wait("G", "sV")
                dmaG("dX1", x1v[:, :, tt * 512:(tt + 1) * 512],
                     x1sl.rearrange("q (k w) -> q k w", k=DC))
            nc.all_engine_barrier()

            p.wait("S", "dX1")
            ln_phase(lambda tt: x1v[:, :, tt * 512:(tt + 1) * 512])

            with p.loop(T4) as tt:
                xf = a32(XF_O, DC * 512)
                p.wait("S", "sV")
                dmaS("dXf", xf.rearrange("q (k w) -> q k w", k=DC),
                     x1v[:, :, dsl(0, tt, 512)])
                xstg = a16(XSTG_O, DC * 512)
                p.wait("V", "sT")
                xb4 = xbn[:].rearrange("q (k t) -> q k t", k=DC)
                p.inc(VV.tensor_copy(xstg.rearrange("q (k w) -> q k w", k=DC),
                                     xb4[:, :, dsl(0, tt, 512)]), "sV")

                def xst(k):
                    return xstg[:, k * 512:(k + 1) * 512]
                p.wait("T", "sA")
                p.wait("T", "sV")
                for k in range(DC):
                    ins = TT.matmul(pX[0:16, :], rww[:, k * 16:(k + 1) * 16],
                                    xst(k), start=(k == 0), stop=(k == DC - 1))
                p.inc(ins, "sT")
                lgf = a32(LG_O, 512)
                p.wait("A", "sT")
                p.wait("A", "dY")
                p.inc(AA.activation(lgf[0:16, :], pX[0:16, :], AF.Copy), "sA")
                rt = a32(RT_O, 384)
                p.wait("T", "sA")
                for tok in range(4):
                    if tok > 0:
                        p.wait("T", "sA")
                    p.inc(TT.transpose(pY[0:128, 0:16],
                                       lgf[0:16, tok * 128:(tok + 1) * 128],
                                       idt32[0:16, 0:16]), "sT")
                    p.wait("A", "sT")
                    p.inc(AA.activation(rt[:, tok * 16:(tok + 1) * 16],
                                        pY[0:128, 0:16], AF.Sigmoid), "sA")
                p.wait("V", "sA")
                _routing(p, VV, nc, rt, cbb[:, :])
                cmbf = a16(CMBF_O, 512)
                p.wait("T", "sV")
                for tok in range(4):
                    if tok > 0:
                        p.wait("T", "sA")
                    p.inc(TT.transpose(pY[0:16, 0:128],
                                       rt[:, 128 + tok * 16:128 + (tok + 1) * 16],
                                       idt32[:, :]), "sT")
                    p.wait("A", "sT")
                    p.inc(AA.activation(cmbf[0:16, tok * 128:(tok + 1) * 128],
                                        pY[0:16, 0:128], AF.Copy), "sA")
                bcall = a16(BCALL_O, E * 512)
                p.wait("G", "sA")
                if use_repl_dma:
                    for e in range(E):
                        dmaG("dRow", bcall[:, e * 512:(e + 1) * 512],
                             cmbf[e:e + 1, :].broadcast_to((128, 512)))
                else:
                    p.wait("G", "sA")
                    for e in range(E):
                        dmaG("dRow", bcall[64:65, e * 512:(e + 1) * 512],
                             cmbf[e:e + 1, :])
                    p.wait("T", "dRow")
                    for e in range(E):
                        ps = pX if e % 2 == 0 else pY
                        if e >= 2:
                            p.wait("T", "sA", -1)
                        p.inc(TT.matmul(ps[:, :], ones_r16[64:65, :],
                                        bcall[64:65, e * 512:(e + 1) * 512],
                                        start=True, stop=True), "sT")
                        p.wait("A", "sT")
                        p.inc(AA.activation(bcall[:, e * 512:(e + 1) * 512],
                                            ps[:, :], AF.Copy), "sA")
                for m in range(MB):
                    pm = m % 2
                    mo = a16(MSG_O[pm], 2 * D)
                    if m >= 2:
                        p.wait("S", "sT", -2)
                    dmaS(f"dMsg{pm}", mo, wb_ffn[:, m * 2 * D:(m + 1) * 2 * D])
                    p.wait("T", f"dMsg{pm}")
                    p.wait("T", "sA")
                    for k in range(DC):
                        ins = TT.matmul(pX[:, :], mo[:, k * 128:(k + 1) * 128],
                                        xst(k), start=(k == 0),
                                        stop=(k == DC - 1))
                    p.inc(ins, "sT")
                    p.wait("T", "sV")
                    for k in range(DC):
                        ins = TT.matmul(pY[:, :],
                                        mo[:, D + k * 128:D + (k + 1) * 128],
                                        xst(k), start=(k == 0),
                                        stop=(k == DC - 1))
                    p.inc(ins, "sT")
                    ft1 = a16(FT1_O, 512)
                    p.wait("A", "sT", -1)
                    p.wait("A", "sV")
                    p.inc(AA.activation(ft1, pX[:, :], AF.Silu), "sA")
                    p.wait("V", "sA")
                    p.wait("V", "sT")
                    p.inc(VV.tensor_tensor(ar16[:, SH_O + m * 512:SH_O + (m + 1) * 512],
                                           ft1, pY[:, :], op=ALU.mult), "sV")
                for e in range(E):
                    pe = e % 2
                    eb = a16(EB_O[pe], EPIECE)
                    if e >= 2:
                        p.wait("S", "sT", -2)
                    dmaS(f"dEb{pe}", eb, wb_ffn[:, SEC1 + e * EPIECE:SEC1 + (e + 1) * EPIECE])
                    p.wait("T", f"dEb{pe}")
                    if pe == 0:
                        gb = (pd2a[:, 0:512], pd2a[:, 512:1024])
                        ub = (pd2b[:, 0:512], pd2b[:, 512:1024])
                        gsil = (pd2a[:, 0:EC * 512],)
                    else:
                        gb = (pA0[:, :], pA1[:, :])
                        ub = (pX[:, :], pY[:, :])
                        gsil = (pA0[:, :], pA1[:, :])[:EC]
                    if e >= 2:
                        p.wait("T", "sA", -1)
                    for mt in range(EC):
                        for k in range(DC):
                            ins = TT.matmul(
                                gb[mt], eb[:, k * EH + mt * 128:k * EH + (mt + 1) * 128],
                                xst(k), start=(k == 0), stop=(k == DC - 1))
                    p.inc(ins, "sT")
                    if e >= 2:
                        p.wait("T", "sV", -1)
                    for mt in range(EC):
                        for k in range(DC):
                            ins = TT.matmul(
                                ub[mt], eb[:, DC * EH + k * EH + mt * 128:
                                           DC * EH + k * EH + (mt + 1) * 128],
                                xst(k), start=(k == 0), stop=(k == DC - 1))
                    p.inc(ins, "sT")
                    ft1 = a16(FT1_O, EC * 512)
                    p.wait("A", "sT", -1)
                    p.wait("A", "sV")
                    if pe == 0:
                        p.inc(AA.activation(ft1, gsil[0], AF.Silu), "sA")
                    else:
                        for gi, gpc in enumerate(gsil):
                            ins = AA.activation(ft1[:, gi * 512:(gi + 1) * 512],
                                                gpc, AF.Silu)
                        p.inc(ins, "sA")
                    p.wait("V", "sA")
                    p.wait("V", "sT")
                    for mt in range(EC):
                        VV.tensor_tensor(ft1[:, mt * 512:(mt + 1) * 512],
                                         ft1[:, mt * 512:(mt + 1) * 512],
                                         ub[mt], op=ALU.mult)
                    for mt in range(EC):
                        ins = VV.tensor_tensor(
                            ar16[:, HW_O + (e * EC + mt) * 512:
                                 HW_O + (e * EC + mt + 1) * 512],
                            ft1[:, mt * 512:(mt + 1) * 512],
                            ar16[:, BCALL_O + e * 512:BCALL_O + (e + 1) * 512],
                            op=ALU.mult)
                    p.inc(ins, "sV")
                for ct in range(DC):
                    pc = ct % 2
                    cb = a16(CBF_O[pc], SEC3C)
                    if ct >= 2:
                        p.wait("S", "sT", -1)
                    dmaS(f"dCb2{pc}", cb, wb_ffn[:, SEC1 + SEC2 + ct * SEC3C:
                                                 SEC1 + SEC2 + (ct + 1) * SEC3C])
                    p.wait("T", f"dCb2{pc}")
                    pA = pA0 if pc == 0 else pA1
                    if ct < 2:
                        p.wait("T", "sV")
                    else:
                        p.wait("T", "sV", -1)
                    for m in range(MB):
                        TT.matmul(pA[:, :], cb[:, m * 128:(m + 1) * 128],
                                  ar16[:, SH_O + m * 512:SH_O + (m + 1) * 512],
                                  start=(m == 0), stop=False)
                    for e in range(E):
                        for mt in range(EC):
                            last = (e == E - 1 and mt == EC - 1)
                            ins = TT.matmul(
                                pA[:, :],
                                cb[:, HS + (e * EC + mt) * 128:
                                   HS + (e * EC + mt + 1) * 128],
                                ar16[:, HW_O + (e * EC + mt) * 512:
                                     HW_O + (e * EC + mt + 1) * 512],
                                start=False, stop=last)
                    p.inc(ins, "sT")
                    p.wait("V", "sT")
                    if ct == 0:
                        p.wait("V", "dXf")
                        p.wait("V", "dY")
                    p.inc(VV.tensor_tensor(
                        ar32[:, OB_O + ct * 512:OB_O + (ct + 1) * 512], pA[:, :],
                        ar32[:, XF_O + ct * 512:XF_O + (ct + 1) * 512],
                        op=ALU.add), "sV")
                p.wait("G", "sV")
                yv = y_out.rearrange("(sq k q) t -> sq q k t", q=128, k=DC)
                ob = a32(OB_O, DC * 512)
                dmaG("dY", yv[dsl(0, s, 1), :, :, dsl(0, tt, 512)],
                     ob.rearrange("q (k w) -> q k w", k=DC))
            nc.all_engine_barrier()

        if dump_internal:
            nd = 0
            for off, n in ((SH_O, MB * 512), (HW_O, E * EC * 512),
                           (BCALL_O, E * 512), (XSTG_O, DC * 512)):
                dmaG("dY", dbg16[:, nd:nd + n], ar16[:, off:off + n])
                nd += n
            dmaG("dY", dbg32[:, 0:384], ar32[:, RT_O:RT_O + 384])
            dmaG("dY", dbg32[:, 384:384 + 512], ar32[:, LG_O:LG_O + 512])
        p.wait("G", "dY")
    return nc


def _routing(p, VV, nc, rt, cbb):
    from concourse.alu_op_type import AluOpType as A

    def VS(ins):
        p.inc(ins, "sV")
        p.wait("V", "sV")
        return ins

    sc32 = rt[:, 0:64]
    sb = rt[:, 64:128]
    cur = rt[:, 128:192]
    gam = rt[:, 192:208]
    tA = rt[:, 208:212]; tB = rt[:, 212:216]
    ta = rt[:, 216:220]; tb = rt[:, 220:224]
    th = rt[:, 224:228]
    ge = rt[:, 228:232]; sel = rt[:, 232:236]; tmp1 = rt[:, 236:240]
    pen = rt[:, 240:256]
    mx = rt[:, 256:260]; dn = rt[:, 260:264]; rcp = rt[:, 264:268]
    msk = rt[:, 272:336]
    tre = rt[:, 336:368]
    r3 = lambda x: x.rearrange("q (r g j) -> q r g j", g=4, j=4)
    r2 = lambda x, g=4: x.rearrange("q (r g) -> q r g", g=g)
    sbv = r3(sb)
    VS(VV.tensor_tensor(sb, sc32, cbb, op=A.add))
    A_ = r2(tA, 1); B_ = r2(tB, 1); a_ = r2(ta, 1); b_ = r2(tb, 1)
    gamv = r2(gam)
    p1 = r2(pen)
    q1 = r2(msk[:, 0:16])
    p2 = r2(msk[:, 16:32])
    q2 = r2(msk[:, 32:48])
    VS(VV.tensor_tensor(p1, sbv[:, :, :, 0], sbv[:, :, :, 1], op=A.max))
    VS(VV.tensor_tensor(q1, sbv[:, :, :, 0], sbv[:, :, :, 1], op=A.min))
    VS(VV.tensor_tensor(p2, sbv[:, :, :, 2], sbv[:, :, :, 3], op=A.max))
    VS(VV.tensor_tensor(q2, sbv[:, :, :, 2], sbv[:, :, :, 3], op=A.min))
    VS(VV.tensor_tensor(gamv, p1, p2, op=A.add))
    VS(VV.tensor_tensor(p1, p1, q1, op=A.add))
    VS(VV.tensor_tensor(p2, p2, q2, op=A.add))
    VS(VV.tensor_tensor(p1, p1, p2, op=A.max))
    VS(VV.tensor_tensor(gamv, gamv, p1, op=A.max))
    g_ = lambda j: gamv[:, :, j:j + 1]
    VS(VV.tensor_tensor(A_, g_(0), g_(1), op=A.max))
    VS(VV.tensor_tensor(a_, g_(0), g_(1), op=A.min))
    VS(VV.tensor_tensor(B_, g_(2), g_(3), op=A.max))
    VS(VV.tensor_tensor(b_, g_(2), g_(3), op=A.min))
    thv = r2(th, 1)
    gev = r2(ge, 1); selv = r2(sel, 1); t1v = r2(tmp1, 1)
    VS(VV.tensor_tensor(thv, A_, B_, op=A.min))
    VS(VV.tensor_tensor(gev, A_, B_, op=A.is_ge))
    VS(VV.tensor_tensor(selv, gev, a_, op=A.mult))
    VS(VV.tensor_scalar(tmp1, ge, 1.0, -1.0, op0=A.subtract, op1=A.mult))
    VS(VV.tensor_tensor(t1v, t1v, b_, op=A.mult))
    VS(VV.tensor_tensor(selv, selv, t1v, op=A.add))
    VS(VV.tensor_tensor(thv, thv, selv, op=A.max))
    gmv = r2(msk[:, 48:64])
    VS(VV.tensor_tensor(gmv, gamv, thv.broadcast_to((128, 4, 4)), op=A.is_ge))
    penv = r2(pen)
    VS(VV.tensor_scalar(pen, msk[:, 48:64], 1.0, BIG, op0=A.subtract, op1=A.mult))
    curv3 = r3(cur)
    for j in range(4):
        VS(VV.tensor_tensor(curv3[:, :, :, j], sbv[:, :, :, j], penv, op=A.add))
    VS(VV.tensor_copy(sb, cur))
    curv = r2(cur, 16)
    mxv = r2(mx, 1)
    trev = r2(tre, 8)
    mskv = r2(msk, 16)

    def rmax16(dst):
        VS(VV.tensor_tensor(trev, curv[:, :, 0:8], curv[:, :, 8:16], op=A.max))
        VS(VV.tensor_tensor(trev[:, :, 0:4], trev[:, :, 0:4], trev[:, :, 4:8],
                            op=A.max))
        VS(VV.tensor_tensor(trev[:, :, 0:2], trev[:, :, 0:2], trev[:, :, 2:4],
                            op=A.max))
        VS(VV.tensor_tensor(dst, trev[:, :, 0:1], trev[:, :, 1:2], op=A.max))

    for rnd in range(3):
        rmax16(mxv)
        VS(VV.tensor_tensor(mskv, curv, mxv.broadcast_to((128, 4, 16)), op=A.is_ge))
        VS(VV.scalar_tensor_tensor(cur, msk, -BIG, cur, op0=A.mult, op1=A.add))
    rmax16(mxv)
    VS(VV.tensor_tensor(mskv, r2(sb, 16), mxv.broadcast_to((128, 4, 16)),
                        op=A.is_ge))
    VS(VV.tensor_tensor(cur, msk, sc32, op=A.mult))
    dnv = r2(dn, 1)
    VS(VV.tensor_tensor(trev, curv[:, :, 0:8], curv[:, :, 8:16], op=A.add))
    VS(VV.tensor_tensor(trev[:, :, 0:4], trev[:, :, 0:4], trev[:, :, 4:8], op=A.add))
    VS(VV.tensor_tensor(trev[:, :, 0:2], trev[:, :, 0:2], trev[:, :, 2:4], op=A.add))
    VS(VV.tensor_tensor(dnv, trev[:, :, 0:1], trev[:, :, 1:2], op=A.add))
    VS(VV.tensor_scalar(dn, dn, 1e-20, None, op0=A.add))
    VS(VV.reciprocal(rcp, dn))
    VS(VV.tensor_tensor(curv, curv, r2(rcp, 1).broadcast_to((128, 4, 16)),
                        op=A.mult))



def _host_layouts(cfg, inputs):
    NSEQ, T, D, H, HD, E, EH, HS = (cfg[k] for k in
                                    ("NSEQ", "T", "D", "H", "HD", "E", "EH", "HS"))
    DC, HP, HHD = D // 128, H // 2, H * HD
    HC = HHD // 128
    MB = HS // 128
    EC = EH // 128
    W_in = np.asarray(inputs["L0_attn_W_in"], np.float32)
    b_in = np.asarray(inputs["L0_attn_b_in"], np.float32)
    W_out = np.asarray(inputs["L0_attn_W_out"], np.float32)
    b_out = np.asarray(inputs["L0_attn_b_out"], np.float32)
    rw = np.asarray(inputs["L0_ffn_router_w"], np.float32)
    cb = np.asarray(inputs["L0_ffn_correction_bias"], np.float32)
    gw = np.asarray(inputs["L0_ffn_gate_w"], np.float32)
    uw = np.asarray(inputs["L0_ffn_up_w"], np.float32)
    dw = np.asarray(inputs["L0_ffn_down_w"], np.float32)
    sg = np.asarray(inputs["L0_ffn_shared_gate_w"], np.float32)
    su = np.asarray(inputs["L0_ffn_shared_up_w"], np.float32)
    sd = np.asarray(inputs["L0_ffn_shared_down_w"], np.float32)

    def lhsT_cols(Wt):
        K, n = Wt.shape
        return np.ascontiguousarray(
            Wt.reshape(K // 128, 128, n).transpose(1, 0, 2).reshape(128, -1))

    w4 = W_in.reshape(D, H, 4, HD)
    sec = {si: np.ascontiguousarray(w4[:, :, si, :].reshape(D, HHD))
           for si in range(4)}
    qkg_parts = []
    for hp in range(HP):
        for si in (0, 1, 3):
            qkg_parts.append(lhsT_cols(sec[si][:, hp * 128:(hp + 1) * 128]))
    wf_qkg = np.concatenate(qkg_parts, axis=1)
    wf_v = lhsT_cols(sec[2])
    wf_out = lhsT_cols(W_out)
    wf_rw = lhsT_cols(rw.T)

    b4 = b_in.reshape(H, 4, HD)
    bsec = {si: np.ascontiguousarray(b4[:, si, :].reshape(HHD)) for si in range(4)}
    col = lambda b: np.ascontiguousarray(b.reshape(-1, 128).T)
    bq_col, bk_col, bg_col = col(bsec[0]), col(bsec[1]), col(bsec[3])
    HPn = bq_col.shape[1]
    bqkg_col = np.zeros((128, HPn * 3), np.float32)
    for hp in range(HPn):
        bqkg_col[:, hp * 3 + 0] = bq_col[:, hp]
        bqkg_col[:, hp * 3 + 1] = bk_col[:, hp]
        bqkg_col[:, hp * 3 + 2] = bg_col[:, hp]
    bv_bc = np.ascontiguousarray(np.broadcast_to(bsec[2], (128, HHD)))
    bo_col = np.ascontiguousarray(b_out.reshape(DC, 128).T)
    cb_bc = np.ascontiguousarray(np.broadcast_to(np.tile(cb, 4), (128, 64)))

    parts = []
    for m in range(MB):
        parts.append(lhsT_cols(np.ascontiguousarray(sg.T[:, m * 128:(m + 1) * 128])))
        parts.append(lhsT_cols(np.ascontiguousarray(su.T[:, m * 128:(m + 1) * 128])))
    for e in range(E):
        parts.append(lhsT_cols(gw[e]))
        parts.append(lhsT_cols(uw[e]))
    sdT = np.ascontiguousarray(sd.T)
    for ct in range(DC):
        parts.append(lhsT_cols(np.ascontiguousarray(sdT[:, ct * 128:(ct + 1) * 128])))
        for e in range(E):
            parts.append(lhsT_cols(np.ascontiguousarray(
                dw[e][:, ct * 128:(ct + 1) * 128])))
    wf_ffn = np.concatenate(parts, axis=1)

    return dict(wf_qkg=wf_qkg, wf_v=wf_v, wf_out=wf_out, wf_rw=wf_rw, wf_ffn=wf_ffn,
                bqkg_col=bqkg_col, bv_bc=bv_bc,
                bo_col=bo_col, cb_bc=cb_bc,
                idm32=np.eye(128, dtype=np.float32))


def kernel(**inputs):
    cfg = CFG_FULL
    NSEQ, T, D = cfg["NSEQ"], cfg["T"], cfg["D"]
    X = np.asarray(inputs["L0_attn_X"], np.float32)
    B = X.shape[0]
    ncores = 8
    assert B == ncores * NSEQ
    host = _host_layouts(cfg, inputs)
    nc = build_program(cfg)
    in_maps = []
    for c in range(ncores):
        xs = X[c * NSEQ:(c + 1) * NSEQ].transpose(0, 2, 1)
        m = dict(host)
        m["x_in"] = np.ascontiguousarray(xs.reshape(NSEQ * D, T))
        in_maps.append(m)
    res = run_bass_kernel_spmd(nc, in_maps, core_ids=list(range(ncores)))
    outs = []
    for c in range(ncores):
        y = res.results[c]["y_out"].reshape(NSEQ, D, T).transpose(0, 2, 1)
        outs.append(y)
    return np.ascontiguousarray(np.concatenate(outs, axis=0))
